# version 01
# baseline (speedup 1.0000x reference)
from contextlib import ExitStack
import numpy as np
import ml_dtypes

import concourse.bass as bass
import concourse.mybir as mybir
from concourse.bass_utils import run_bass_kernel_spmd

F32 = mybir.dt.float32
BF16 = mybir.dt.bfloat16
AF = mybir.ActivationFunctionType
ALU = mybir.AluOpType
AX = mybir.AxisListType
BF = ml_dtypes.bfloat16

NB, S, D = 2, 4096, 4096
CH = 1024
HALO = 4
TW = CH + HALO
NQH, NKVH, HD = 16, 4, 64
LAT, KVD = 1024, 256
EPS = 1e-5

ENGS = ("T", "V", "S", "G", "Y")



def _mk_perm(n_heads):
    perm = np.zeros(n_heads * HD, dtype=np.int64)
    for h in range(n_heads):
        blk, a = h // 2, h % 2
        for i in range(HD):
            pos = (a * 32 + i // 2) if i % 2 == 0 else (64 + a * 32 + (i - 1) // 2)
            perm[blk * 128 + pos] = h * HD + i
    return perm


PQ = np.arange(NQH * HD)
PK = np.arange(NKVH * HD)


def _conv_block_weights(w, perm, n_blk):
    out = np.zeros((n_blk, 3, 128, 128), np.float32)
    for blk in range(n_blk):
        o = perm[blk * 128:(blk + 1) * 128]
        for pi in range(128):
            same = (o // HD) == (o[pi] // HD)
            idx = np.nonzero(same)[0]
            out[blk, :, pi, idx] = w[o[idx], o[pi] % HD, :]
    return out


def _dense_conv_weights(w, perm, n_ob):
    wp = w[np.ix_(perm, perm)]
    out = np.empty((n_ob, w.shape[1], 3, 128), np.float32)
    for ob in range(n_ob):
        out[ob] = wp[ob * 128:(ob + 1) * 128].transpose(1, 2, 0)
    return np.ascontiguousarray(out)


def prep_shared(a):
    sh = {}
    wq = np.asarray(a["L0_attn_w_qkv"], np.float32)
    wq_p = np.concatenate([wq[:LAT][PQ], wq[LAT:LAT + KVD][PK], wq[LAT + KVD:]], 0)
    wqkv_t = np.ascontiguousarray(wq_p.T)
    sh["wqkv_ob"] = np.ascontiguousarray(wqkv_t.reshape(D, 12, 128).transpose(1, 0, 2))
    sh["c1q"] = _conv_block_weights(np.asarray(a["L0_attn_qc1_w"], np.float32), PQ, 8)
    sh["c1k"] = _conv_block_weights(np.asarray(a["L0_attn_kc1_w"], np.float32), PK, 2)
    sh["c2q"] = _dense_conv_weights(np.asarray(a["L0_attn_qc2_w"], np.float32), PQ, 8)
    sh["c2k"] = _dense_conv_weights(np.asarray(a["L0_attn_kc2_w"], np.float32), PK, 2)
    cb = np.zeros((128, 20), np.float32)
    cb[:, 0:8] = np.asarray(a["L0_attn_qc1_b"], np.float32)[PQ].reshape(8, 128).T
    cb[:, 8:16] = np.asarray(a["L0_attn_qc2_b"], np.float32)[PQ].reshape(8, 128).T
    cb[:, 16:18] = np.asarray(a["L0_attn_kc1_b"], np.float32)[PK].reshape(2, 128).T
    cb[:, 18:20] = np.asarray(a["L0_attn_kc2_b"], np.float32)[PK].reshape(2, 128).T
    sh["cb"] = cb
    wo = np.asarray(a["L0_attn_w_o"], np.float32)
    sh["wo_ob"] = np.ascontiguousarray(wo.T.reshape(LAT, 32, 128).transpose(1, 0, 2))
    wup = np.asarray(a["L0_ffn_W_up"], np.float32)
    sh["wup_ob"] = np.ascontiguousarray(wup.T.reshape(D, 32, 128).transpose(1, 0, 2))
    gza = np.zeros((D, 64), np.float32)
    gza[:, 0:4] = np.asarray(a["L0_ffn_gate_w"], np.float32).T
    gza[:, 32:64] = np.asarray(a["L0_ffn_lora_A"], np.float32).reshape(32, D).T
    sh["gza"] = gza
    eb = np.zeros((1, 64), np.float32)
    eb[0, 0:4] = np.asarray(a["L0_ffn_expert_bias"], np.float32)
    sh["ebias"] = eb
    sh["bstack"] = np.ascontiguousarray(
        np.asarray(a["L0_ffn_lora_B"], np.float32).transpose(0, 2, 1).reshape(32, D))
    sh["ktemp"] = np.ascontiguousarray(np.broadcast_to(
        np.asarray(a["L0_attn_key_temp"], np.float32).reshape(1, 1), (128, 1)))
    return sh


def prep_core(a, core):
    b, j = core // 4, core % 4
    x = np.asarray(a["L0_attn_x"], np.float32)
    t0 = j * CH
    xs = np.zeros((TW, D), np.float32)
    lo = max(0, t0 - HALO)
    xs[HALO - (t0 - lo):] = x[b, lo:t0 + CH]
    inv = 1.0 / 10000.0 ** (np.arange(0, HD, 2, dtype=np.float64) / HD)
    pos = np.arange(t0, t0 + CH, dtype=np.float64)
    ang = (pos[:, None] * inv[None, :]).reshape(8, 128, 32)
    cs = np.cos(ang).transpose(1, 0, 2).reshape(128, 256).astype(np.float32)
    sn = np.sin(ang).transpose(1, 0, 2).reshape(128, 256).astype(np.float32)
    return {
        "xs": xs,
        "cosT": np.ascontiguousarray(cs),
        "sinT": np.ascontiguousarray(sn),
        "hmask": np.full((128, 1), 0.0 if j == 0 else 1.0, np.float32),
    }


def _consts():
    c = {}
    c["ident"] = np.eye(128, dtype=BF)
    p = np.arange(128)
    rowmap = p % 64
    P = np.zeros((128, 64), BF)
    P[p, rowmap] = 1
    c["meanpat"] = P
    kmbc = np.zeros((64, 128), BF)
    kmbc[rowmap, p] = 0.125
    c["kmbc"] = kmbc
    qmbc = np.zeros((64, 128), BF)
    qmbc[rowmap, p] = 0.03125
    c["qmbc"] = qmbc
    L = np.zeros((128, 2), BF)
    L[p < 64, 0] = 1
    L[p >= 64, 1] = 1
    c["l2pat"] = L
    BCm = np.zeros((2, 128), BF)
    BCm[p // 64, p] = 1
    c["l2bc"] = BCm
    ii, jj = np.meshgrid(np.arange(128), np.arange(128), indexing="ij")
    c["stair"] = (jj >= ii).astype(BF)
    Dm = np.zeros((128, 256), BF)
    Dm[32 * (p // 64), p] = 1
    Dm[64 + 32 * (p // 64), 128 + p] = 1
    c["denbc"] = Dm.astype(np.float32)
    c["zeros_bf"] = np.zeros((128, 1024), BF)
    e64 = np.zeros((128, 64), np.float32)
    e64[64, :] = 1
    c["e64"] = e64
    Rep = np.zeros((4, 32), BF)
    Rep[np.arange(32) // 8, np.arange(32)] = 1
    c["repmat"] = Rep
    c["ones_bf"] = np.ones((128, 1024), BF)
    c["ident4"] = np.eye(4, dtype=np.float32)
    fc = np.zeros((128, 4), np.float32)
    fc[:, 0] = EPS
    fc[:, 1] = 1.0
    c["fconst"] = fc
    return c



def _dyn2(base, row, nrow, col, ncol, p):
    ap = base[bass.ds(row, nrow), bass.ds(col, ncol)]
    return ap.rearrange("(t p) c -> p t c", p=p)


def _dynkv(base, row, col):
    ap = base[bass.ds(row, 256), :]
    ap = ap.rearrange("r c -> (r c)").rearrange("(t v) -> t v", v=256)
    ap = ap[:, bass.ds(col, 64)]
    return ap.rearrange("(t p) v -> p t v", p=128)



class Sched:
    def __init__(self, nc, ctx):
        self.nc = nc
        self.ctx = ctx
        self.q = {e: [] for e in ENGS}
        self.m = {e: 0 for e in ENGS}
        self.d = {}
        self.sem_m = {e: ctx.enter_context(nc.semaphore(f"m_{e}")) for e in ENGS}
        self.sem_d = {}
        self.waited = {e: {} for e in ENGS}
        self.banklast = {}

    def wait(self, eng, tok):
        if tok is None:
            return
        kind, pe, cnt = tok
        key = (kind, pe)
        if self.waited[eng].get(key, -1) >= cnt:
            return
        self.waited[eng][key] = cnt
        sem = self.sem_m[pe] if kind == "m" else self.sem_d[pe]
        val = cnt if kind == "m" else 16 * cnt
        self.q[eng].append(lambda e, sem=sem, val=val: e.wait_ge(sem, val))

    def _dsem(self, stream):
        if stream not in self.sem_d:
            self.sem_d[stream] = self.ctx.enter_context(
                self.nc.semaphore(f"d_{stream}"))
            self.d[stream] = 0
        return self.sem_d[stream]

    def op(self, eng, fn, waits=(), inc=False):
        for t in waits:
            self.wait(eng, t)
        if inc:
            self.m[eng] += 1
            sem = self.sem_m[eng]
            cnt = self.m[eng]
            self.q[eng].append(lambda e, fn=fn, sem=sem: fn(e).then_inc(sem, 1))
            self.q[eng].append(lambda e, sem=sem, cnt=cnt: e.wait_ge(sem, cnt))
            return ("m", eng, cnt)
        self.q[eng].append(fn)
        return None

    def dma(self, eng, out, in_, waits=(), stream=None, nobc=False):
        stream = stream or eng
        sem = self._dsem(stream)
        for t in waits:
            self.wait(eng, t)
        self.d[stream] += 1

        idx = self.d[stream]

        def run(e, out=out, in_=in_, sem=sem, stream=stream, idx=idx, nobc=nobc):
            if callable(out):
                out = out()
            if callable(in_):
                in_ = in_()
            kw = {"bounds_check": None} if nobc else {}
            try:
                e.dma_start(out=out, in_=in_, **kw).then_inc(sem, 16)
            except Exception:
                print(f"DMA FAIL stream={stream} idx={idx}")
                raise
        self.q[eng].append(run)
        return ("d", stream, self.d[stream])

    def mark(self, eng):
        self.m[eng] += 1
        sem = self.sem_m[eng]
        cnt = self.m[eng]
        self.q[eng].append(lambda e, sem=sem: e.nop().then_inc(sem, 1))
        self.q[eng].append(lambda e, sem=sem, cnt=cnt: e.wait_ge(sem, cnt))
        return ("m", eng, cnt)

    def bank_waits(self, banks):
        out = []
        for b in banks:
            t = self.banklast.get(b)
            if t is not None:
                out.append(t)
        return out

    def bank_read(self, banks, tok):
        for b in banks:
            self.banklast[b] = tok



def build(mode="full"):
    nc = bass.Bass("TRN2", target_bir_lowering=False, debug=False)
    C = _consts()

    def din(name, shape, dt=F32):
        return nc.dram_tensor(name, shape, dt, kind="ExternalInput").ap()

    xs = din("xs", [TW, D])
    wqkv = din("wqkv_ob", [12, D, 128])
    c1q_d = din("c1q", [8, 3, 128, 128])
    c1k_d = din("c1k", [2, 3, 128, 128])
    c2q_d = din("c2q", [8, LAT, 3, 128])
    c2k_d = din("c2k", [2, KVD, 3, 128])
    cb_d = din("cb", [128, 20])
    cos_d = din("cosT", [128, 256])
    sin_d = din("sinT", [128, 256])
    hm_d = din("hmask", [128, 1])
    kt_d = din("ktemp", [128, 1])

    cst = {k: nc.inline_tensor(np.ascontiguousarray(v), name="c_" + k).ap()
           for k, v in C.items()}

    if mode == "A":
        contrib = nc.dram_tensor("contrib", [1536, CH], BF16, kind="ExternalOutput").ap()
    elif mode == "full":
        contrib = nc.dram_tensor("cc1_in", [1536, CH], BF16, kind="Internal").ap()
    else:
        contrib = None
    if mode == "B":
        agout = nc.dram_tensor("agout", [12288, CH], BF16, kind="ExternalInput").ap()
        attn_out = nc.dram_tensor("attn_out", [256, S], BF16,
                                  kind="ExternalOutput").ap()
    elif mode == "full":
        agout = nc.dram_tensor("cc1_out", [12288, CH], BF16, kind="Internal").ap()
        attn_out = nc.dram_tensor("cc2_in", [256, S], BF16, kind="Internal").ap()
    if mode == "C":
        ag2 = nc.dram_tensor("ag2", [2048, S], BF16, kind="ExternalInput").ap()
    elif mode == "full":
        ag2 = nc.dram_tensor("cc2_out", [2048, S], BF16, kind="Internal").ap()
    if mode in ("C", "full"):
        xout = nc.dram_tensor("xout", [CH, D], F32, kind="ExternalOutput").ap()
        wo_d = nc.dram_tensor("wo_ob", [32, LAT, 128], F32, kind="ExternalInput").ap()
        wup_d = nc.dram_tensor("wup_ob", [32, D, 128], F32, kind="ExternalInput").ap()
        gza_d = nc.dram_tensor("gza", [D, 64], F32, kind="ExternalInput").ap()
        eb_d = nc.dram_tensor("ebias", [1, 64], F32, kind="ExternalInput").ap()
        bst_d = nc.dram_tensor("bstack", [32, D], F32, kind="ExternalInput").ap()
        x2d = nc.dram_tensor("x2d", [CH, D], F32, kind="Internal").ap()
        wyd = nc.dram_tensor("wyd", [D, CH], BF16, kind="Internal").ap()
        fyd = nc.dram_tensor("fyd", [D, CH], BF16, kind="Internal").ap()

    ctx = ExitStack()
    with ctx:
        sbt = lambda name, shape, dt: ctx.enter_context(nc.sbuf_tensor(name, shape, dt))
        x2t = sbt("x2t", [128, 2 * D], F32)
        xn_bf = sbt("xn_bf", [128, 2 * D], BF16)
        xnT = sbt("xnT", [128, 32 * 1032], BF16)
        wf = sbt("wf", [128, 2 * D], F32)
        wb = sbt("wb", [128, 2 * D], BF16)
        qT = sbt("qT", [128, 8 * TW], BF16)
        kT = sbt("kT", [128, 2 * TW], BF16)
        vT = sbt("vT", [128, 2 * TW], BF16)
        c1wq = sbt("c1wq", [128, 8 * 3 * 128], BF16)
        c1wk = sbt("c1wk", [128, 2 * 3 * 128], BF16)
        cbs = sbt("cbs", [128, 20], F32)
        coss = sbt("coss", [128, 256], F32)
        sins = sbt("sins", [128, 256], F32)
        hms = sbt("hms", [128, 1], F32)
        kts = sbt("kts", [128, 1], F32)
        stt = sbt("stt", [128, 40], F32)
        vpc = sbt("vpc", [128, 2 * 128], BF16)
        ks = {}
        for k in C:
            dt = BF16 if C[k].dtype == BF else F32
            ks[k] = sbt("k_" + k, list(C[k].shape), dt)

        c1q_s = xnT[:, 0:8 * 1026]
        c1k_s = xnT[:, 8 * 1026:10 * 1026]
        off = 10 * 1026 + 2
        qa = xnT[:, off:off + 4096].bitcast(F32)
        off += 4096
        q2 = xnT[:, off:off + 1024]
        off += 1024
        ropet = xnT[:, off:off + 4096].bitcast(F32)
        off += 4096
        qout = xnT[:, off:off + 2048]
        off += 2048
        mean_sb = xnT[0:64, off:off + 2048]
        off += 2048
        kmarr = xnT[:, off:off + 2048].bitcast(F32)
        off += 2048
        qmarr = xnT[:, off:off + 2048].bitcast(F32)
        off += 2048
        rr = xnT[0:2, off:off + 4096].bitcast(F32)
        off += 4096
        rbf = xnT[0:2, off:off + 1024]
        off += 1024
        assert off <= 32 * 1032, off

        ps = ctx.enter_context(nc.psum_tensor("ps", [128, 4096], F32))

        def bank(i):
            return ps[:, 512 * i:512 * (i + 1)]

        xnT3 = xnT[:, :].rearrange("p (kb c) -> p kb c", c=1032)

        sch = Sched(nc, ctx)

        for k in ks:
            idx = tuple(slice(None) for _ in C[k].shape)
            sch.dma("Y", ks[k][idx], cst[k][idx], stream="c")
        for srcd, dst in ((cb_d, cbs), (cos_d, coss), (sin_d, sins),
                          (hm_d, hms), (kt_d, kts)):
            sch.dma("Y", dst[tuple(slice(None) for _ in dst.shape)],
                    srcd[:, :], stream="c")
        sch.dma("Y", wf[:, 0:3072].rearrange("p (b s m) -> p b s m", b=8, s=3),
                c1q_d.rearrange("b s p m -> p b s m"), stream="c")
        t_c1fk = sch.dma("Y", wf[:, 3072:3840].rearrange("p (b s m) -> p b s m", b=2, s=3),
                         c1k_d.rearrange("b s p m -> p b s m"), stream="c")
        t_constd = t_c1fk
        sch.op("G", lambda e: e.tensor_copy(out=c1wq[:, :], in_=wf[:, 0:3072]),
               waits=[t_constd])
        t_c1w = sch.op("G", lambda e: e.tensor_copy(out=c1wk[:, :], in_=wf[:, 3072:3840]),
                       inc=True)

        cast_toks = {}
        wdma_toks = {}
        wseq = [0]
        stat_tok = {}

        def wchunk(i, width, src_ap, wb_free_tok, dst3=None):
            dst = wf[:, (i % 2) * D:(i % 2) * D + width]
            if dst3 is not None:
                dst = dst.rearrange("p (a c) -> p a c", a=dst3[0])
            wdma_toks[i] = sch.dma(
                "Y", dst, src_ap,
                waits=([cast_toks[i - 2]] if i - 2 in cast_toks else [t_c1w]),
                stream=f"w{i % 2}")
            w = [wdma_toks[i]]
            if wb_free_tok is not None:
                w.append(wb_free_tok)
            cast_toks[i] = sch.op("G", lambda e, i=i, width=width: e.tensor_copy(
                out=wb[:, (i % 2) * D:(i % 2) * D + width],
                in_=wf[:, (i % 2) * D:(i % 2) * D + width]), waits=w, inc=True)
            return cast_toks[i]


        if mode in ("A", "full"):
            apply_toks = {}
            x_toks = {-1: sch.dma("Y", x2t[0:4, 0:D], xs[0:4, :], stream="x0")}

            def ln_tile(t):
                nrow = 4 if t < 0 else 128
                sl0 = ((t % 2) if t >= 0 else 0) * D
                xt = x2t[0:nrow, sl0:sl0 + D]
                xo = xn_bf[0:nrow, sl0:sl0 + D]
                L = 16 if t < 0 else 8 * (t % 2)
                slot_free = xn_slot_tok.get(t % 2 if t >= 0 else 0)
                t_sq = sch.op("S", lambda e: e.activation(
                    out=xo,
                    in_=xt, func=AF.Square, accum_out=stt[0:nrow, L + 1:L + 2]),
                    waits=[x_toks[t], t_constd] + ([slot_free] if slot_free else []),
                    inc=True)
                t_red = sch.op("V", lambda e: e.tensor_reduce(
                    out=stt[0:nrow, L:L + 1], in_=xt, axis=AX.X, op=ALU.add),
                    waits=[x_toks[t], t_constd], inc=True)
                t_m = sch.op("V", lambda e: e.tensor_scalar(
                    out=stt[0:nrow, L + 2:L + 3], in0=stt[0:nrow, L:L + 1],
                    scalar1=1.0 / D, scalar2=None, op0=ALU.mult),
                    waits=[t_red], inc=True)
                t_msq = sch.op("V", lambda e: e.tensor_tensor(
                    out=stt[0:nrow, L + 3:L + 4], in0=stt[0:nrow, L + 2:L + 3],
                    in1=stt[0:nrow, L + 2:L + 3], op=ALU.mult),
                    waits=[t_m], inc=True)
                t_var = sch.op("V", lambda e: e.tensor_scalar(
                    out=stt[0:nrow, L + 4:L + 5], in0=stt[0:nrow, L + 1:L + 2],
                    scalar1=1.0 / D, scalar2=stt[0:nrow, L + 3:L + 4],
                    op0=ALU.mult, op1=ALU.subtract), waits=[t_sq, t_msq], inc=True)
                t_ln = sch.op("S", lambda e: e.activation(
                    out=stt[0:nrow, L + 5:L + 6], in_=stt[0:nrow, L + 4:L + 5],
                    func=AF.Ln, bias=ks["fconst"][0:nrow, 0:1]), waits=[t_var],
                    inc=True)
                t_rs = sch.op("S", lambda e: e.activation(
                    out=stt[0:nrow, L + 6:L + 7], in_=stt[0:nrow, L + 5:L + 6],
                    func=AF.Exp, scale=-0.5), waits=[t_ln], inc=True)
                t_mn = sch.op("V", lambda e: e.tensor_scalar(
                    out=stt[0:nrow, L + 7:L + 8], in0=stt[0:nrow, L + 2:L + 3],
                    scalar1=-1.0, scalar2=None, op0=ALU.mult), waits=[t_m], inc=True)
                tok = sch.op("V", lambda e: e.tensor_scalar(
                    out=xo, in0=xt, scalar1=stt[0:nrow, L + 7:L + 8],
                    scalar2=stt[0:nrow, L + 6:L + 7], op0=ALU.add, op1=ALU.mult),
                    waits=[t_rs, t_mn] + ([slot_free] if slot_free else []), inc=True)
                apply_toks[t] = tok
                if t < 0:
                    bw = sch.bank_waits([6])
                    for kb in range(32):
                        t_tp = sch.op("T", lambda e, kb=kb: e.matmul(
                            out=ps[:, 512 * 6 + 4 * kb:512 * 6 + 4 * kb + 4],
                            lhsT=xn_bf[0:4, 128 * kb:128 * (kb + 1)],
                            rhs=ks["ident"][0:4, 0:4],
                            start=True, stop=True),
                            waits=(bw + [tok, t_constd]) if kb == 0 else [],
                            inc=(kb == 31))
                    ev = sch.op("V", lambda e: e.tensor_copy(
                        out=xnT3[:, 0:32, 0:4], in_=bank(6)[:, 0:128]),
                        waits=[t_tp], inc=True)
                    sch.bank_read([6], ev)
                else:
                    for g in range(8):
                        pbank = 6 + (g % 2)
                        bw = sch.bank_waits([pbank])
                        for i in range(4):
                            kb = 4 * g + i
                            t_tp = sch.op("T", lambda e, kb=kb, i=i, pbank=pbank: e.matmul(
                                out=ps[:, 512 * pbank + 128 * i:512 * pbank + 128 * (i + 1)],
                                lhsT=xn_bf[0:128, (t % 2) * D + 128 * kb:(t % 2) * D + 128 * (kb + 1)],
                                rhs=ks["ident"][:, :], start=True, stop=True),
                                waits=(bw + [tok]) if i == 0 else [], inc=(i == 3))
                        ev = sch.op("V", lambda e, g=g, t=t, pbank=pbank: e.tensor_copy(
                            out=xnT3[:, 4 * g:4 * g + 4, 4 + t * 128:4 + (t + 1) * 128],
                            in_=bank(pbank)[:, 0:512]), waits=[t_tp], inc=True)
                        sch.bank_read([pbank], ev)
                xn_slot_tok[t % 2 if t >= 0 else 0] = ("m", "T", sch.m["T"])
                if 0 <= t < 6:
                    x_toks[t + 2] = sch.dma("Y", x2t[:, (t % 2) * D:(t % 2 + 1) * D],
                                            xs[4 + (t + 2) * 128:4 + (t + 3) * 128, :],
                                            waits=[apply_toks[t], t_sq],
                                            stream=f"x{t % 2}")

            xn_slot_tok = {}
            ln_tile(-1)
            x_toks[0] = sch.dma("Y", x2t[:, 0:D], xs[4:132, :],
                                waits=[apply_toks[-1], xn_slot_tok[0]], stream="x0")
            x_toks[1] = sch.dma("Y", x2t[:, D:2 * D], xs[132:260, :], stream="x1")
            for t in range(8):
                ln_tile(t)
            t_xnT = ("m", "V", sch.m["V"])

            qkv_mm = {}
            qkv_evac = {}
            for ob in range(12):
                t_cast = sch_wchunk_qkv = wchunk(
                    ob, D, wqkv[ob].rearrange("(kb p) c -> p kb c", p=128),
                    qkv_mm.get(ob - 2), dst3=(32, 128))
                lo = 0 if ob % 2 == 0 else 3
                bw = sch.bank_waits([lo, lo + 1, lo + 2])
                for kb in range(32):
                    w = []
                    if kb == 0:
                        w = bw + [t_cast, t_xnT]
                    for piece, (o, wd, bk) in enumerate(
                            ((0, 512, lo), (512, 512, lo + 1), (1024, 4, lo + 2))):
                        tk_mm = sch.op("T", lambda e, ob=ob, kb=kb, o=o, wd=wd, bk=bk: e.matmul(
                            out=ps[:, 512 * bk:512 * bk + wd],
                            lhsT=wb[:, (ob % 2) * D + 128 * kb:(ob % 2) * D + 128 * (kb + 1)],
                            rhs=xnT3[:, kb, o:o + wd],
                            start=(kb == 0), stop=(kb == 31)),
                            waits=w if piece == 0 else [],
                            inc=(kb == 31 and piece == 2))
                qkv_mm[ob] = tk_mm
                if ob < 8:
                    dst, base = qT, ob * TW
                elif ob < 10:
                    dst, base = kT, (ob - 8) * TW
                else:
                    dst, base = vT, (ob - 10) * TW
                sch.op("V", lambda e, lo=lo, dst=dst, base=base: e.tensor_copy(
                    out=dst[:, base:base + 512], in_=bank(lo)[:, :]),
                    waits=[qkv_mm[ob]])
                sch.op("V", lambda e, lo=lo, dst=dst, base=base: e.tensor_copy(
                    out=dst[:, base + 512:base + 1024], in_=bank(lo + 1)[:, :]))
                qkv_evac[ob] = sch.op("V", lambda e, lo=lo, dst=dst, base=base: e.tensor_copy(
                    out=dst[:, base + 1024:base + 1028], in_=bank(lo + 2)[:, 0:4]), inc=True)
                sch.bank_read([lo, lo + 1, lo + 2], qkv_evac[ob])

            bw = sch.bank_waits([0, 1])
            for b2 in range(2):
                for piece in range(2):
                    t_km = sch.op("T", lambda e, b2=b2, piece=piece: e.matmul(
                        out=bank(piece)[0:64, :], lhsT=ks["meanpat"][:, :],
                        rhs=kT[:, b2 * TW + 4 + 512 * piece:b2 * TW + 4 + 512 * (piece + 1)],
                        start=(b2 == 0), stop=(b2 == 1)),
                        waits=(bw + [qkv_evac[9], t_constd]) if (b2 == 0 and piece == 0) else [],
                        inc=(b2 == 1 and piece == 1))
            for piece in range(2):
                tk = sch.op("V", lambda e, piece=piece: e.tensor_copy(
                    out=mean_sb[:, 512 * piece:512 * (piece + 1)],
                    in_=bank(piece)[0:64, :]), waits=[t_km], inc=(piece == 1))
            t_km_sb = tk
            sch.bank_read([0, 1], t_km_sb)
            bw = sch.bank_waits([2, 3])
            for qb in range(8):
                for piece in range(2):
                    t_qm = sch.op("T", lambda e, qb=qb, piece=piece: e.matmul(
                        out=bank(2 + piece)[0:64, :], lhsT=ks["meanpat"][:, :],
                        rhs=qT[:, qb * TW + 4 + 512 * piece:qb * TW + 4 + 512 * (piece + 1)],
                        start=(qb == 0), stop=(qb == 7)),
                        waits=(bw + [qkv_evac[7], t_constd]) if (qb == 0 and piece == 0) else [],
                        inc=(qb == 7 and piece == 1))
            for piece in range(2):
                tk = sch.op("V", lambda e, piece=piece: e.tensor_copy(
                    out=mean_sb[:, CH + 512 * piece:CH + 512 * (piece + 1)],
                    in_=bank(2 + piece)[0:64, :]), waits=[t_qm], inc=(piece == 1))
            t_qm_sb = tk
            sch.bank_read([2, 3], t_qm_sb)
            bw = sch.bank_waits([4, 5, 6, 7])
            for piece in range(2):
                sch.op("T", lambda e, piece=piece: e.matmul(
                    out=bank(4 + piece)[:, :], lhsT=ks["kmbc"][:, :],
                    rhs=mean_sb[:, 512 * piece:512 * (piece + 1)],
                    start=True, stop=True),
                    waits=(bw + [t_km_sb]) if piece == 0 else [])
                t_bc = sch.op("T", lambda e, piece=piece: e.matmul(
                    out=bank(6 + piece)[:, :], lhsT=ks["qmbc"][:, :],
                    rhs=mean_sb[:, CH + 512 * piece:CH + 512 * (piece + 1)],
                    start=True, stop=True), waits=[t_qm_sb] if piece == 0 else [],
                    inc=(piece == 1))
            for piece in range(2):
                sch.op("V", lambda e, piece=piece: e.tensor_copy(
                    out=kmarr[:, 512 * piece:512 * (piece + 1)],
                    in_=bank(4 + piece)[:, :]), waits=[t_bc])
                tk = sch.op("V", lambda e, piece=piece: e.tensor_copy(
                    out=qmarr[:, 512 * piece:512 * (piece + 1)],
                    in_=bank(6 + piece)[:, :]), inc=(piece == 1))
            t_marr = tk
            sch.bank_read([4, 5, 6, 7], t_marr)

            c1_done = {}
            for isq, (nblk, srcT, wsb, dst, bcol) in enumerate(
                    ((8, qT, c1wq, c1q_s, 0), (2, kT, c1wk, c1k_s, 16))):
                for b3 in range(nblk):
                    pl = (b3 % 2) * 3
                    bw = sch.bank_waits([pl, pl + 1, pl + 2])
                    for tap in range(3):
                        for o, wd, bk in ((0, 512, pl), (512, 512, pl + 1),
                                          (1024, 2, pl + 2)):
                            w = []
                            if tap == 0 and o == 0:
                                w = bw + [t_c1w, t_marr, qkv_evac[11]]
                            t_mm = sch.op("T", lambda e, b3=b3, tap=tap, o=o, wd=wd, bk=bk,
                                   srcT=srcT, wsb=wsb: e.matmul(
                                out=ps[:, 512 * bk:512 * bk + wd],
                                lhsT=wsb[:, (b3 * 3 + tap) * 128:(b3 * 3 + tap + 1) * 128],
                                rhs=srcT[:, b3 * TW + tap + o:b3 * TW + tap + o + wd],
                                start=(tap == 0), stop=(tap == 2)), waits=w,
                                inc=(tap == 2 and o == 1024))
                    t_ev = sch.op("S", lambda e, b3=b3, pl=pl, dst=dst, bcol=bcol: e.activation(
                        out=dst[:, b3 * 1026:(b3 + 1) * 1026],
                        in_=ps[:, 512 * pl:512 * pl + 1026],
                        func=AF.Identity, bias=cbs[:, bcol + b3:bcol + b3 + 1]),
                        waits=[t_mm], inc=True)
                    sch.bank_read([pl, pl + 1, pl + 2], t_ev)
                    c1_done[(isq, b3)] = sch.op("V", lambda e, b3=b3, dst=dst: e.tensor_scalar(
                        out=dst[:, b3 * 1026:b3 * 1026 + 2],
                        in0=dst[:, b3 * 1026:b3 * 1026 + 2],
                        scalar1=hms[:, 0:1], scalar2=None, op0=ALU.mult),
                        waits=[t_ev], inc=True)
            t_c1_all = c1_done[(1, 1)]

            prev_bc_tok = [None]
            qa_slot_tok = {}
            qout_slot_tok = {}

            def assembly(isq, nblk, c2dram, c1src, dst_row0, marr, bcol, wi_base):
                asm = {}
                for ob in range(nblk):
                    wi = wi_base + ob
                    width = 3072 if isq == 0 else 768
                    t_cast = wchunk(wi, width,
                                    c2dram[ob].rearrange("(cb p) s c -> p cb s c", p=128)
                                    .rearrange("p cb s c -> p cb (s c)"),
                                    ("m", "T", sch.m["T"]), dst3=(8 if isq == 0 else 2, 384))
                    ncb = 8 if isq == 0 else 2
                    pl = (ob % 2) * 2
                    bw = sch.bank_waits([pl, pl + 1])
                    for tc in range(2):
                        for cib in range(ncb):
                            for tap in range(3):
                                w = []
                                if tc == 0 and cib == 0 and tap == 0:
                                    w = bw + [t_cast, t_c1_all]
                                t_c2 = sch.op("T", lambda e, tc=tc, cib=cib, tap=tap, pl=pl,
                                       wi=wi, c1src=c1src: e.matmul(
                                    out=bank(pl + tc)[:, :],
                                    lhsT=wb[:, (wi % 2) * D + (cib * 3 + tap) * 128:
                                            (wi % 2) * D + (cib * 3 + tap + 1) * 128],
                                    rhs=c1src[:, cib * 1026 + tc * 512 + tap:
                                              cib * 1026 + tc * 512 + tap + 512],
                                    start=(cib == 0 and tap == 0),
                                    stop=(cib == ncb - 1 and tap == 2)), waits=w,
                                    inc=(tc == 1 and cib == ncb - 1 and tap == 2))
                    slot = (ob % 2) * 1024
                    qa_free = qa_slot_tok.get(slot)
                    c2b = 8 if isq == 0 else 18
                    t_ev = sch.op("S", lambda e, pl=pl, slot=slot, ob=ob, c2b=c2b: e.activation(
                        out=qa[:, slot:slot + 1024], in_=ps[:, 512 * pl:512 * pl + 1024],
                        func=AF.Identity, bias=cbs[:, c2b + ob:c2b + ob + 1]),
                        waits=[t_c2] + ([qa_free] if qa_free else []),
                        inc=True)
                    sch.bank_read([pl, pl + 1], t_ev)
                    t_ad = sch.op("V", lambda e, slot=slot, marr=marr: e.tensor_tensor(
                        out=qa[:, slot:slot + 1024], in0=qa[:, slot:slot + 1024],
                        in1=marr[:, :], op=ALU.add), waits=[t_ev], inc=True)
                    t_q2 = sch.op("S", lambda e, slot=slot: e.activation(
                        out=q2[:, :], in_=qa[:, slot:slot + 1024], func=AF.Square),
                        waits=[t_ad] + ([prev_bc_tok[0]] if prev_bc_tok[0] else []),
                        inc=True)
                    bw2 = sch.bank_waits([4, 5, 6])
                    for piece in range(2):
                        t_l2 = sch.op("T", lambda e, piece=piece: e.matmul(
                            out=bank(4 + piece)[0:2, :], lhsT=ks["l2pat"][:, :],
                            rhs=q2[:, 512 * piece:512 * (piece + 1)],
                            start=True, stop=True),
                            waits=(bw2 + [t_q2]) if piece == 0 else [],
                            inc=(piece == 1))
                    for piece in range(2):
                        tk = sch.op("V", lambda e, piece=piece: e.tensor_copy(
                            out=rr[:, 512 * piece:512 * (piece + 1)],
                            in_=bank(4 + piece)[0:2, :]), waits=[t_l2], inc=(piece == 1))
                    t_rrc = tk
                    sch.bank_read([4, 5], t_rrc)
                    t_n = sch.op("S", lambda e: e.activation(
                        out=rr[:, CH:2 * CH], in_=rr[:, 0:CH], func=AF.Sqrt),
                        waits=[t_rrc], inc=True)
                    t_mx = sch.op("V", lambda e: e.tensor_scalar(
                        out=rr[:, CH:2 * CH], in0=rr[:, CH:2 * CH],
                        scalar1=1e-12, scalar2=None, op0=ALU.max), waits=[t_n],
                        inc=True)
                    t_rc = sch.op("V", lambda e: e.reciprocal(
                        out=rr[:, 0:CH], in_=rr[:, CH:2 * CH]), waits=[t_mx],
                        inc=True)
                    if isq == 1:
                        t_rc = sch.op("V", lambda e: e.tensor_scalar(
                            out=rr[:, 0:CH], in0=rr[:, 0:CH],
                            scalar1=kts[0:2, 0:1], scalar2=None, op0=ALU.mult),
                            waits=[t_rc], inc=True)
                    t_rbf = sch.op("V", lambda e: e.tensor_copy(
                        out=rbf[:, :], in_=rr[:, 0:CH]), waits=[t_rc], inc=True)
                    bw3 = sch.bank_waits([5, 6])
                    for piece in range(2):
                        t_bcm = sch.op("T", lambda e, piece=piece: e.matmul(
                            out=bank(5 + piece)[:, :], lhsT=ks["l2bc"][:, :],
                            rhs=rbf[:, 512 * piece:512 * (piece + 1)],
                            start=True, stop=True),
                            waits=(bw3 + [t_rbf]) if piece == 0 else [],
                            inc=(piece == 1))
                    prev_bc_tok[0] = t_bcm
                    for piece in range(2):
                        tk = sch.op("V", lambda e, piece=piece, slot=slot: e.tensor_tensor(
                            out=q2[:, 512 * piece:512 * (piece + 1)],
                            in0=qa[:, slot + 512 * piece:slot + 512 * (piece + 1)],
                            in1=bank(5 + piece)[:, :], op=ALU.mult),
                            waits=[t_bcm, t_ad], inc=True)
                    t_qn = tk
                    sch.bank_read([5, 6], t_qn)
                    qa_slot_tok[slot] = t_qn
                    last = None
                    prev_rp = qa_slot_tok.get("rope")
                    for tt in range(8):
                        tb = 4 if tt % 2 == 0 else 7
                        bw4 = sch.bank_waits([tb])
                        tmm = sch.op("T", lambda e, tt=tt, tb=tb: e.matmul(
                            out=bank(tb)[:, 0:128],
                            lhsT=q2[:, tt * 128:(tt + 1) * 128],
                            rhs=ks["ident"][:, :], start=True, stop=True),
                            waits=bw4 + [t_qn], inc=True)
                        pc = bank(tb)[:, 0:128].rearrange("p (h x two) -> p h x two",
                                                          two=2, x=32)
                        qe = pc[:, :, :, 0]
                        qo = pc[:, :, :, 1]
                        def _bc2(apx):
                            return bass.AP(apx.tensor, apx.offset,
                                           [list(apx.ap[0]), [0, 2], list(apx.ap[1])])
                        cA = _bc2(coss[:, tt * 32:(tt + 1) * 32])
                        sA = _bc2(sins[:, tt * 32:(tt + 1) * 32])
                        osl = (ob % 2) * 1024 + tt * 128
                        oc = qout[:, osl:osl + 128].rearrange(
                            "p (h x two) -> p h x two", two=2, x=32)
                        qfree = qout_slot_tok.get(ob % 2)
                        t1 = ropet[:, 0:64].rearrange("p (h x) -> p h x", h=2)
                        t2 = ropet[:, 64:128].rearrange("p (h x) -> p h x", h=2)
                        w1 = sch.op("V", lambda e, qe=qe, cA=cA, t1=t1: e.tensor_tensor(
                            out=t1, in0=qe, in1=cA, op=ALU.mult),
                            waits=[tmm] + ([qfree] if qfree else [])
                            + ([prev_rp] if prev_rp else []), inc=True)
                        w2 = sch.op("V", lambda e, qo=qo, sA=sA, t2=t2: e.tensor_tensor(
                            out=t2, in0=qo, in1=sA, op=ALU.mult), inc=True)
                        w3 = sch.op("V", lambda e, oc=oc, t1=t1, t2=t2: e.tensor_tensor(
                            out=oc[:, :, :, 0], in0=t1, in1=t2, op=ALU.subtract),
                            waits=[w1, w2], inc=True)
                        w4 = sch.op("V", lambda e, qe=qe, sA=sA, t1=t1: e.tensor_tensor(
                            out=t1, in0=qe, in1=sA, op=ALU.mult), waits=[w3], inc=True)
                        w5 = sch.op("V", lambda e, qo=qo, cA=cA, t2=t2: e.tensor_tensor(
                            out=t2, in0=qo, in1=cA, op=ALU.mult), waits=[w3], inc=True)
                        t_rp = sch.op("V", lambda e, oc=oc, t1=t1, t2=t2: e.tensor_tensor(
                            out=oc[:, :, :, 1], in0=t1, in1=t2, op=ALU.add),
                            waits=[w4, w5], inc=True)
                        prev_rp = t_rp
                        qa_slot_tok["rope"] = t_rp
                        sch.bank_read([tb], t_rp)
                        if isq == 0:
                            dst = contrib[tt * 128:(tt + 1) * 128,
                                          ob * 128:(ob + 1) * 128]
                        else:
                            kreg = contrib[1024:1280, :].rearrange(
                                "r c -> (r c)").rearrange("(t v) -> t v", v=256)
                            dst = kreg[tt * 128:(tt + 1) * 128,
                                       ob * 128:(ob + 1) * 128]
                        last = sch.dma("G", dst, qout[:, osl:osl + 128],
                                       waits=[t_rp], stream=f"o{ob % 2}")
                    asm[ob] = last
                    qout_slot_tok[ob % 2] = last
                return asm

            q_asm = assembly(0, 8, c2q_d, c1q_s, 0, kmarr, 0, 12)
            k_asm = assembly(1, 2, c2k_d, c1k_s, 1024, qmarr, 8, 20)
            wseq[0] = 22

            vreg = contrib[1280:1536, :].rearrange("r c -> (r c)").rearrange(
                "(t v) -> t v", v=256)
            v_ev = {}
            for i in range(16):
                vb, t = i // 8, i % 8
                coff = (4 if vb == 0 else 3) + t * 128
                slot = (i % 2) * 128
                bk = 7
                bw = sch.bank_waits([bk])
                tmm = sch.op("T", lambda e, vb=vb, coff=coff, slot=slot, bk=bk: e.matmul(
                    out=ps[:, 512 * bk + slot:512 * bk + slot + 128],
                    lhsT=vT[:, vb * TW + coff:vb * TW + coff + 128],
                    rhs=ks["ident"][:, :], start=True, stop=True),
                    waits=bw + [qkv_evac[11]], inc=True)
                tev = sch.op("V", lambda e, slot=slot, bk=bk: e.tensor_copy(
                    out=vpc[:, slot:slot + 128],
                    in_=ps[:, 512 * bk + slot:512 * bk + slot + 128]),
                    waits=[tmm] + ([v_ev[i - 2]] if i - 2 in v_ev else []), inc=True)
                sch.bank_read([bk], tev)
                v_ev[i] = sch.dma("G", vreg[t * 128:(t + 1) * 128, vb * 128:(vb + 1) * 128],
                                  vpc[:, slot:slot + 128], waits=[tev],
                                  stream=f"v{i % 2}")

            if mode == "A":
                for st in ("o0", "o1", "v0", "v1"):
                    if st in sch.d:
                        sch.wait("Y", ("d", st, sch.d[st]))


        if mode in ("B", "full"):
            dsl = bass.ds
            qg = xnT[:, 0:8192]
            kg = xnT[:, 8192:10240]
            vgo = xnT[:, 10240:12320]
            QT01 = xnT[:, 12320:16416]
            QT23 = xnT[:, 16416:20512]
            KT2 = xnT[:, 20512:24608]
            att = xnT[:, 24608:28704]
            rec_f = xnT[:, 28704:32800].bitcast(F32)
            ob_out = xnT[:, 0:2048]
            bc_f = xnT[:, 2048:6144].bitcast(F32)
            regs = {"G": {}, "Y": {}, "S": {}}

            def _mk_offs(e, eng):
                r = regs[eng]
                pid = e.partition_id()
                r["rowb"] = e.snap((pid // 4) * 6144, min_val=0, max_val=6144)
                r["qcol"] = e.snap((pid % 4) * 256, min_val=0, max_val=768)
                r["kcol"] = e.snap((pid % 4) * 64, min_val=0, max_val=192)
                return e.nop()
            offs_tok = {}
            for eng in ("G", "Y", "S"):
                offs_tok[eng] = sch.op(eng, lambda e, eng=eng: _mk_offs(e, eng),
                                       waits=[apply_toks[7]] if mode == "full" and eng in ("Y", "S") else [],
                                       inc=True)

            for j in range(4):
                sch.dma(
                    "G",
                    qg.rearrange("p (t c) -> p t c", c=256)[:, 8 * j:8 * j + 8, :],
                    lambda j=j: _dyn2(agout, regs["G"]["rowb"] + 1536 * j, 1024,
                                      regs["G"]["qcol"], 256, 128),
                    waits=[offs_tok["G"]], stream="gq", nobc=True)
                sch.dma(
                    "Y",
                    kg.rearrange("p (t v) -> p t v", v=64)[:, 8 * j:8 * j + 8, :],
                    lambda j=j: _dynkv(agout, regs["Y"]["rowb"] + 1536 * j + 1024,
                                       regs["Y"]["kcol"]),
                    waits=[offs_tok["Y"]], stream="gk", nobc=True)
                sch.dma(
                    "S",
                    vgo.rearrange("p (t w) -> p t w", w=65)[:, 8 * j:8 * j + 8, 0:64],
                    lambda j=j: _dynkv(agout, regs["S"]["rowb"] + 1536 * j + 1280,
                                       regs["S"]["kcol"]),
                    waits=[offs_tok["S"]], stream="gv", nobc=True)

            for zz in range(4):
                sch.dma("G", xnT[:, 28704 + 1024 * zz:29728 + 1024 * zz],
                        cst["zeros_bf"][:, :], stream="gq")
            t_vones = sch.op("G", lambda e: e.tensor_copy(
                out=vgo.rearrange("p (t w) -> p t w", w=65)[:, :, 64:65],
                in_=ks["ones_bf"][:, 0:32].rearrange("p (c o) -> p c o", o=1)),
                waits=[t_constd], inc=True)
            t_gall = None
            for st_ in ("gq", "gk", "gv"):
                t_gall = ("d", st_, sch.d[st_])
                sch.wait("T", t_gall)
                sch.wait("V", t_gall)
            t_gall = ("d", "gq", sch.d["gq"])

            t_ab = sch.op("S", lambda e: e.activation(
                out=stt[:, 8:9], in_=kts[:, 0:1], func=AF.Abs, scale=0.125),
                waits=[t_constd], inc=True)
            t_mneg = sch.op("V", lambda e: e.tensor_scalar(
                out=stt[:, 9:10], in0=stt[:, 8:9], scalar1=-1.0, scalar2=None,
                op0=ALU.mult), waits=[t_ab], inc=True)
            mneg = stt[:, 9:10]

            tp_ev = {}
            for grp in range(8):
                pb_ = grp % 2
                for dst_i, dstQ in ((0, QT01), (1, QT23)):
                    bw = sch.bank_waits([pb_ + 2 * dst_i])
                    for i in range(4):
                        tt = grp * 4 + i
                        sch.op("T", lambda e, tt=tt, i=i, pb_=pb_, dst_i=dst_i: e.matmul(
                            out=ps[:, 512 * (pb_ + 2 * dst_i) + 128 * i:
                                   512 * (pb_ + 2 * dst_i) + 128 * (i + 1)],
                            lhsT=qg[:, tt * 256 + 128 * dst_i:tt * 256 + 128 * (dst_i + 1)],
                            rhs=ks["ident"][:, :], start=True, stop=True),
                            waits=(bw + [t_gall]) if i == 0 else [],
                            inc=(i == 3))
                    t_t = ("m", "T", sch.m["T"])
                    ev = sch.op("V", lambda e, grp=grp, pb_=pb_, dst_i=dst_i,
                                dstQ=dstQ: e.tensor_copy(
                        out=dstQ[:, grp * 512:grp * 512 + 512],
                        in_=bank(pb_ + 2 * dst_i)[:, :]), waits=[t_t], inc=True)
                    sch.bank_read([pb_ + 2 * dst_i], ev)
                    tp_ev[(grp, dst_i)] = ev
            for grp in range(8):
                pb_ = 4 + grp % 2
                bw = sch.bank_waits([pb_])
                for i in range(4):
                    tt = grp * 4 + i
                    sch.op("T", lambda e, tt=tt, i=i, pb_=pb_: e.matmul(
                        out=ps[:, 512 * pb_ + 128 * i:512 * pb_ + 128 * (i + 1)][0:64, :],
                        lhsT=kg[:, tt * 64:(tt + 1) * 64],
                        rhs=ks["ident"][:, :], start=True, stop=True),
                        waits=(bw + [t_gall]) if i == 0 else [])
                    sch.op("T", lambda e, tt=tt, i=i, pb_=pb_: e.matmul(
                        out=ps[:, 512 * pb_ + 128 * i:512 * pb_ + 128 * (i + 1)][64:128, :],
                        lhsT=kg[:, tt * 64:(tt + 1) * 64],
                        rhs=ks["ident"][:, :], start=True, stop=True,
                        tile_position=(0, 64)), inc=(i == 3))
                t_t = ("m", "T", sch.m["T"])
                ev = sch.op("V", lambda e, grp=grp, pb_=pb_: e.tensor_copy(
                    out=KT2[:, grp * 512:(grp + 1) * 512],
                    in_=bank(pb_)[:, :]), waits=[t_t], inc=True)
                sch.bank_read([pb_], ev)
                tp_ev[("k", grp)] = ev
            t_tpall = ("m", "V", sch.m["V"])

            exp_tok = {}
            avdone = {}
            out_dma = {}
            rec_tok = None
            for qt in range(8):
                nkb = 4 * qt + 4
                for kb in range(nkb + 1):
                    if kb < nkb:
                        ql0 = max(0, kb * 128 - qt * 512)
                        diag = kb >= 4 * qt
                        for pair, (QTp, hh) in enumerate(((QT01, 0), (QT23, 2))):
                            w = sch.bank_waits([hh, hh + 1])
                            for h in (hh, hh + 1):
                                w += ([exp_tok[(qt, kb - 1, h)]] if (qt, kb - 1, h) in exp_tok else [])
                            if qt == 0 and kb == 0 and pair == 0:
                                w.append(t_tpall)
                            for i2, h in enumerate((hh, hh + 1)):
                                base_p = 64 * i2
                                sch.op("T", lambda e, kb=kb, qt=qt, ql0=ql0, h=h,
                                       base_p=base_p, QTp=QTp: e.matmul(
                                    out=bank(h)[:, ql0:512],
                                    lhsT=KT2[base_p:base_p + 64, kb * 128:(kb + 1) * 128],
                                    rhs=QTp[base_p:base_p + 64,
                                            qt * 512 + ql0:qt * 512 + 512],
                                    start=True, stop=True,
                                    tile_position=(base_p, 0)),
                                    waits=w if i2 == 0 else [], inc=(i2 == 1))
                            t_sc = ("m", "T", sch.m["T"])
                            for i2, h in enumerate((hh, hh + 1)):
                                slot = (h * 2 + (kb % 2)) * 512
                                tk = sch.op("S", lambda e, h=h, ql0=ql0, slot=slot: e.activation(
                                    out=att[:, slot + ql0:slot + 512],
                                    in_=bank(h)[:, ql0:512], func=AF.Exp,
                                    scale=0.125, bias=mneg),
                                    waits=[t_sc] + ([avdone[(qt, kb - 2)]]
                                                    if (qt, kb - 2) in avdone and h == hh else []),
                                    inc=True)
                                sch.bank_read([h], tk)
                                if diag:
                                    tk = sch.op("V", lambda e, h=h, ql0=ql0, slot=slot: e.tensor_tensor(
                                        out=att[:, slot + ql0:slot + ql0 + 128],
                                        in0=att[:, slot + ql0:slot + ql0 + 128],
                                        in1=ks["stair"][:, :], op=ALU.mult),
                                        waits=[tk], inc=True)
                                exp_tok[(qt, kb, h)] = tk
                    if kb >= 1:
                        kbp = kb - 1
                        ql0p = max(0, kbp * 128 - qt * 512)
                        for h in range(4):
                            slot = (h * 2 + (kbp % 2)) * 512
                            w = [exp_tok[(qt, kbp, h)]]
                            if kbp == 0:
                                w += sch.bank_waits([4 + h]) + [t_vones]
                            sch.op("T", lambda e, h=h, ql0p=ql0p, slot=slot,
                                   kbp=kbp, nkb=nkb: e.matmul(
                                out=bank(4 + h)[0:65, ql0p:512],
                                lhsT=vgo[:, kbp * 65:kbp * 65 + 65],
                                rhs=att[:, slot + ql0p:slot + 512],
                                start=(kbp == 0), stop=(kbp == nkb - 1)),
                                waits=w, inc=(h == 3))
                        avdone[(qt, kbp)] = ("m", "T", sch.m["T"])
                t_av = avdone[(qt, nkb - 1)]
                evtoks = []
                for h in range(4):
                    w = [t_av]
                    if rec_tok is not None and h == 0:
                        w.append(rec_tok)
                    evtoks.append(sch.op("V", lambda e, h=h, qt=qt: e.tensor_copy(
                        out=rec_f[64:65, 512 * h:512 * h + 512],
                        in_=bank(4 + h)[64:65, :]), waits=w, inc=True))
                t_rc = sch.op("V", lambda e: e.reciprocal(
                    out=rec_f[64:65, :], in_=rec_f[64:65, :]),
                    waits=[evtoks[-1]], inc=True)
                for h in range(4):
                    bw = sch.bank_waits([h])
                    t_bcm = sch.op("T", lambda e, h=h: e.matmul(
                        out=bank(h)[0:64, :], lhsT=ks["e64"][:, :],
                        rhs=rec_f[:, 512 * h:512 * h + 512].bitcast(BF16) if False
                        else rec_f[:, 512 * h:512 * h + 512],
                        start=True, stop=True), waits=bw + [t_rc], inc=True)
                    t_bce = sch.op("V", lambda e, h=h: e.tensor_copy(
                        out=bc_f[0:64, 512 * h:512 * h + 512],
                        in_=bank(h)[0:64, :]), waits=[t_bcm], inc=True)
                    w2 = [t_bce]
                    if (qt - 1, h) in out_dma:
                        w2.append(out_dma[(qt - 1, h)])
                    t_ob = sch.op("V", lambda e, h=h: e.tensor_tensor(
                        out=ob_out[0:64, 512 * h:512 * h + 512],
                        in0=bank(4 + h)[0:64, :],
                        in1=bc_f[0:64, 512 * h:512 * h + 512], op=ALU.mult),
                        waits=w2, inc=True)
                    sch.bank_read([h], t_ob)
                    sch.bank_read([4 + h], t_ob)
                    out_dma[(qt, h)] = sch.dma(
                        "G", attn_out[64 * h:64 * (h + 1), qt * 512:(qt + 1) * 512],
                        ob_out[0:64, 512 * h:512 * h + 512], waits=[t_ob],
                        stream=f"ao{h}")
                rec_tok = ("m", "T", sch.m["T"])
            if mode == "B":
                for st_ in ("ao0", "ao1", "ao2", "ao3"):
                    sch.wait("Y", ("d", st_, sch.d[st_]))

        if mode in ("C", "full"):
            dsl = bass.ds
            latT = xnT[:, 0:8192]
            ybuf = xnT[:, 8192:12288]
            ysq = xnT[:, 12288:13312]
            srow = xnT[0:1, 13312:21504].bitcast(F32)
            hnT = xnT[:, 0:32768]
            hnT3 = hnT.rearrange("p (kb c) -> p kb c", c=1024)
            fbuf = qT[:, 0:2048]
            fsq = qT[:, 2048:3072]
            frow = srow
            zsel = qT[:, 7168:8192]
            gza_bf = vT[:, 0:2048]
            ohT_sb = kT[0:4, 0:1024]
            oh_tok = kT[:, 1024:1056]
            eb_bf = kT[0:1, 1056:1120]
            xpc = x2t[:, 0:1024]
            x2pc = x2t[:, 1024:2048]
            outpc = x2t[:, 2048:3072]
            tpc = x2t[:, 3072:4096]
            bst_bf = qT[:, 3072:7168]
            gz_sb = qT[0:64, 0:2048].bitcast(F32)

            regs2 = {}

            def _mk_offs2(e):
                pid = e.partition_id()
                regs2["row2"] = e.snap((pid // 4) * 1024, min_val=0, max_val=1024)
                regs2["col2"] = e.snap((pid % 4) * 1024, min_val=0, max_val=3072)
                return e.nop()
            if mode == "C":
                w_pre = []
            else:
                w_pre = [("d", "cc2", sch.d.get("cc2", 0))]
            t_off2 = sch.op("G", lambda e: _mk_offs2(e), inc=True)
            t_lat = sch.dma(
                "G", latT.rearrange("p (kb c) -> p kb c", c=1024),
                lambda: ag2[dsl(regs2["row2"], 1024), dsl(regs2["col2"], 1024)]
                .rearrange("(kb p) c -> p kb c", p=128),
                waits=[t_off2] + w_pre, stream="g2", nobc=True)

            def wload(width, src_ap, cast_dst, rows=None, dst3=None):
                i = wseq[0]
                wseq[0] += 1
                sl = (i % 2) * D
                dst = wf[:, sl:sl + width] if rows is None else wf[rows[0]:rows[1], sl:sl + width]
                ddst = dst if dst3 is None else dst.rearrange(
                    "p (a c) -> p a c", a=dst3)
                tok_d = sch.dma("Y", ddst, src_ap,
                                waits=([cast_toks[i - 2]] if i - 2 in cast_toks
                                       else [t_c1w]),
                                stream=f"w{i % 2}")
                cast_toks[i] = sch.op("G", lambda e: e.tensor_copy(
                    out=cast_dst,
                    in_=dst), waits=[tok_d, ("m", "T", sch.m["T"])], inc=True)
                return cast_toks[i]

            t_gzac = wload(2048, gza_d.rearrange("(kb p) c -> p kb c", p=128),
                           gza_bf[:, :], dst3=32)
            t_ebc = wload(64, eb_d[:, :], eb_bf[:, :], rows=(0, 1))
            t_bstc = wload(4096, bst_d[:, :], bst_bf[32:64, :], rows=(32, 64))

            yev = {}
            for ob in range(32):
                i = wseq[0]
                t_cast = wchunk(i, 1024,
                                wo_d[ob].rearrange("(kb p) c -> p kb c", p=128),
                                ("m", "T", sch.m["T"]), dst3=(8, 128))
                wseq[0] += 1
                pl = (ob % 2) * 2
                bw = sch.bank_waits([pl, pl + 1])
                for kb in range(8):
                    for piece in range(2):
                        sch.op("T", lambda e, i=i, kb=kb, piece=piece, pl=pl: e.matmul(
                            out=bank(pl + piece)[:, :],
                            lhsT=wb[:, (i % 2) * D + 128 * kb:(i % 2) * D + 128 * (kb + 1)],
                            rhs=latT[:, 1024 * kb + 512 * piece:1024 * kb + 512 * (piece + 1)],
                            start=(kb == 0), stop=(kb == 7)),
                            waits=(bw + [t_cast, t_lat]) if kb == 0 and piece == 0 else [],
                            inc=(kb == 7 and piece == 1))
                t_mm = ("m", "T", sch.m["T"])
                ysl = (ob % 2) * 1024
                t_ev = sch.op("V", lambda e, pl=pl, ysl=ysl: e.tensor_copy(
                    out=ybuf[:, ysl:ysl + 1024],
                    in_=ps[:, 512 * pl:512 * pl + 1024]),
                    waits=[t_mm] + ([yev[ob - 2]] if ob - 2 in yev else []), inc=True)
                sch.bank_read([pl, pl + 1], t_ev)
                t_sq = sch.op("S", lambda e, ysl=ysl: e.activation(
                    out=ysq[:, :], in_=ybuf[:, ysl:ysl + 1024], func=AF.Square),
                    waits=[t_ev] + ([stat_tok[ob - 1]] if ob - 1 in stat_tok else []),
                    inc=True)
                bw2 = sch.bank_waits([4, 5, 6, 7]) if ob == 0 else []
                for piece in range(2):
                    sch.op("T", lambda e, piece=piece, ysl=ysl, ob=ob: e.matmul(
                        out=bank(4 + piece)[0:1, :], lhsT=ks["ones_bf"][:, 0:1],
                        rhs=ybuf[:, ysl + 512 * piece:ysl + 512 * (piece + 1)],
                        start=(ob == 0), stop=(ob == 31)),
                        waits=(bw2 + [t_ev]) if piece == 0 else [])
                for piece in range(2):
                    sch.op("T", lambda e, piece=piece, ob=ob: e.matmul(
                        out=bank(6 + piece)[0:1, :], lhsT=ks["ones_bf"][:, 0:1],
                        rhs=ysq[:, 512 * piece:512 * (piece + 1)],
                        start=(ob == 0), stop=(ob == 31)),
                        waits=[t_sq], inc=(piece == 1))
                stat_tok[ob] = ("m", "T", sch.m["T"])
                yev[ob] = sch.dma("G", wyd[128 * ob:128 * (ob + 1), :],
                                  ybuf[:, ysl:ysl + 1024], waits=[t_ev],
                                  stream=f"sy{ob % 2}")

            def row_stats(sum_banks, sq_banks, row, lanes, wtok):
                evs = []
                for piece in range(2):
                    evs.append(sch.op("V", lambda e, piece=piece: e.tensor_copy(
                        out=row[:, 512 * piece:512 * (piece + 1)],
                        in_=bank(sum_banks[piece])[0:1, :]), waits=[wtok],
                        inc=True))
                    evs.append(sch.op("V", lambda e, piece=piece: e.tensor_copy(
                        out=row[:, 1024 + 512 * piece:1024 + 512 * (piece + 1)],
                        in_=bank(sq_banks[piece])[0:1, :]), inc=True))
                for b_ in sum_banks + sq_banks:
                    sch.bank_read([b_], evs[-1])
                t1 = sch.op("V", lambda e: e.tensor_scalar(
                    out=row[:, 2048:3072], in0=row[:, 0:1024], scalar1=1.0 / D,
                    scalar2=None, op0=ALU.mult), waits=[evs[-1]], inc=True)
                t2 = sch.op("V", lambda e: e.tensor_tensor(
                    out=row[:, 3072:4096], in0=row[:, 2048:3072],
                    in1=row[:, 2048:3072], op=ALU.mult), waits=[t1], inc=True)
                t3 = sch.op("V", lambda e: e.tensor_scalar(
                    out=row[:, 0:1024], in0=row[:, 1024:2048], scalar1=1.0 / D,
                    scalar2=None, op0=ALU.mult), waits=[t2], inc=True)
                t4 = sch.op("V", lambda e: e.tensor_tensor(
                    out=row[:, 0:1024], in0=row[:, 0:1024],
                    in1=row[:, 3072:4096], op=ALU.subtract), waits=[t3], inc=True)
                t5 = sch.op("S", lambda e: e.activation(
                    out=row[:, 0:1024], in_=row[:, 0:1024], func=AF.Ln,
                    bias=ks["fconst"][0:1, 0:1]), waits=[t4], inc=True)
                t6 = sch.op("S", lambda e: e.activation(
                    out=row[:, 0:1024], in_=row[:, 0:1024], func=AF.Exp,
                    scale=-0.5), waits=[t5], inc=True)
                t7 = sch.op("V", lambda e: e.tensor_scalar(
                    out=row[:, 2048:3072], in0=row[:, 2048:3072], scalar1=-1.0,
                    scalar2=None, op0=ALU.mult), waits=[t6], inc=True)
                bw = sch.bank_waits([0])
                for sub in range(8):
                    sch.op("T", lambda e, sub=sub: e.matmul(
                        out=bank(0)[:, 2 * sub:2 * sub + 1],
                        lhsT=row[:, 2048 + 128 * sub:2048 + 128 * (sub + 1)],
                        rhs=ks["fconst"][0:1, 1:2], start=True, stop=True),
                        waits=(bw + [t7]) if sub == 0 else [])
                    sch.op("T", lambda e, sub=sub: e.matmul(
                        out=bank(0)[:, 2 * sub + 1:2 * sub + 2],
                        lhsT=row[:, 128 * sub:128 * (sub + 1)],
                        rhs=ks["fconst"][0:1, 1:2], start=True, stop=True),
                        inc=(sub == 7))
                t8 = ("m", "T", sch.m["T"])
                t9 = sch.op("V", lambda e: e.tensor_copy(
                    out=stt[:, lanes:lanes + 16].rearrange(
                        "p (s two) -> p s two", two=2) if False else stt[:, lanes:lanes + 16],
                    in_=bank(0)[:, 0:16]), waits=[t8], inc=True)
                sch.bank_read([0], t9)
                return t9

            stat_tok = stat_tok
            t_ln2 = row_stats([4, 5], [6, 7], srow, 0, stat_tok[31])

            def combine_pass(ydram, row_lanes, res_src, res_rows_off, dst_dram,
                             stats_tok, rd_stream, xs_stream, out_stream,
                             slot_free0):
                rdev = {}
                cmb = {}
                xslot_tok = {}
                for obg in range(8):
                    for obi in range(4):
                        ob = obg * 4 + obi
                        ysl = (ob % 2) * 1024
                        w = [stats_tok]
                        if ob - 2 in rdev:
                            w.append(rdev[ob - 2])
                        elif ob < 2:
                            w.append(slot_free0[ob % 2])
                        t_rd = sch.dma("Y", ybuf[:, ysl:ysl + 1024],
                                       ydram[128 * ob:128 * (ob + 1), :],
                                       waits=w, stream=f"{rd_stream}{ob % 2}")
                        for sub in range(8):
                            bw = sch.bank_waits([sub]) if obi == 0 else []
                            rdev[ob] = sch.op("T", lambda e, ob=ob, obi=obi,
                                              sub=sub, ysl=ysl: e.matmul(
                                out=bank(sub)[:, 128 * obi:128 * (obi + 1)],
                                lhsT=ybuf[:, ysl + 128 * sub:ysl + 128 * (sub + 1)],
                                rhs=ks["ident"][:, :], start=True, stop=True),
                                waits=(bw + [t_rd]) if True else [],
                                inc=(sub == 7))
                    t_tp = ("m", "T", sch.m["T"])
                    for sub in range(8):
                        xsl = (sub % 2) * 512
                        w = []
                        if xslot_tok.get(sub % 2) is not None:
                            w.append(xslot_tok[sub % 2])
                        t_x = sch.dma("Y", xpc[:, xsl:xsl + 512],
                                      res_src[res_rows_off + 128 * sub:
                                              res_rows_off + 128 * (sub + 1),
                                              512 * obg:512 * (obg + 1)],
                                      waits=w, stream=f"{xs_stream}{sub % 2}")
                        t_af = sch.op("V", lambda e, sub=sub, xsl=xsl: e.tensor_scalar(
                            out=tpc[:, xsl:xsl + 512], in0=bank(sub)[:, :],
                            scalar1=stt[:, row_lanes + 2 * sub:row_lanes + 2 * sub + 1],
                            scalar2=stt[:, row_lanes + 2 * sub + 1:row_lanes + 2 * sub + 2],
                            op0=ALU.add, op1=ALU.mult),
                            waits=[t_tp] + w, inc=True)
                        sch.bank_read([sub], t_af)
                        t_cm = sch.op("V", lambda e, sub=sub, xsl=xsl: e.tensor_tensor(
                            out=outpc[:, xsl:xsl + 512], in0=tpc[:, xsl:xsl + 512],
                            in1=xpc[:, xsl:xsl + 512], op=ALU.add),
                            waits=[t_af, t_x], inc=True)
                        cmb[(obg, sub)] = sch.dma(
                            "G", dst_dram[128 * sub:128 * (sub + 1),
                                          512 * obg:512 * (obg + 1)],
                            outpc[:, xsl:xsl + 512], waits=[t_cm],
                            stream=f"{out_stream}{sub % 2}")
                        xslot_tok[sub % 2] = cmb[(obg, sub)]
                return cmb

            cmb2 = combine_pass(wyd, 0, xs, 4, x2d, t_ln2, "ry", "rx", "sx",
                                {0: yev[30], 1: yev[31]})
            t_x2_all = [cmb2[(7, s)] for s in range(8)]

            ap3_toks = {}
            h_slot_tok = {}
            for t in range(8):
                L = 24 + 8 * (t % 2)
                sl0 = (t % 2) * D
                xt = x2t[0:128, sl0:sl0 + D]
                xo = xn_bf[0:128, sl0:sl0 + D]
                w = [cmb2[(g, t)] for g in range(8)]
                if t < 2:
                    w += [cmb2[(7, s)] for s in range(8)]
                if t - 2 in ap3_toks:
                    w.append(ap3_toks[t - 2])
                t_xr = sch.dma("Y", xt, x2d[128 * t:128 * (t + 1), :], waits=w,
                               stream=f"x2r{t % 2}")
                slot_free = h_slot_tok.get(t % 2)
                t_sq = sch.op("S", lambda e, xo=xo, xt=xt, L=L: e.activation(
                    out=xo, in_=xt, func=AF.Square, accum_out=stt[:, L + 1:L + 2]),
                    waits=[t_xr] + ([slot_free] if slot_free else []), inc=True)
                t_red = sch.op("V", lambda e, xt=xt, L=L: e.tensor_reduce(
                    out=stt[:, L:L + 1], in_=xt, axis=AX.X, op=ALU.add),
                    waits=[t_xr], inc=True)
                t_m = sch.op("V", lambda e, L=L: e.tensor_scalar(
                    out=stt[:, L + 2:L + 3], in0=stt[:, L:L + 1],
                    scalar1=1.0 / D, scalar2=None, op0=ALU.mult),
                    waits=[t_red], inc=True)
                t_msq = sch.op("V", lambda e, L=L: e.tensor_tensor(
                    out=stt[:, L + 3:L + 4], in0=stt[:, L + 2:L + 3],
                    in1=stt[:, L + 2:L + 3], op=ALU.mult), waits=[t_m], inc=True)
                t_var = sch.op("V", lambda e, L=L: e.tensor_scalar(
                    out=stt[:, L + 4:L + 5], in0=stt[:, L + 1:L + 2],
                    scalar1=1.0 / D, scalar2=stt[:, L + 3:L + 4],
                    op0=ALU.mult, op1=ALU.subtract), waits=[t_sq, t_msq], inc=True)
                t_ln = sch.op("S", lambda e, L=L: e.activation(
                    out=stt[:, L + 5:L + 6], in_=stt[:, L + 4:L + 5],
                    func=AF.Ln, bias=ks["fconst"][:, 0:1]), waits=[t_var], inc=True)
                t_rs = sch.op("S", lambda e, L=L: e.activation(
                    out=stt[:, L + 6:L + 7], in_=stt[:, L + 5:L + 6],
                    func=AF.Exp, scale=-0.5), waits=[t_ln], inc=True)
                t_mn = sch.op("V", lambda e, L=L: e.tensor_scalar(
                    out=stt[:, L + 7:L + 8], in0=stt[:, L + 2:L + 3],
                    scalar1=-1.0, scalar2=None, op0=ALU.mult), waits=[t_m], inc=True)
                t_ap = sch.op("V", lambda e, xo=xo, xt=xt, L=L: e.tensor_scalar(
                    out=xo, in0=xt, scalar1=stt[:, L + 7:L + 8],
                    scalar2=stt[:, L + 6:L + 7], op0=ALU.add, op1=ALU.mult),
                    waits=[t_rs, t_mn] + ([slot_free] if slot_free else []),
                    inc=True)
                ap3_toks[t] = t_ap
                for g in range(8):
                    pbank = 6 + (g % 2)
                    bw = sch.bank_waits([pbank])
                    for i in range(4):
                        kb = 4 * g + i
                        t_tp = sch.op("T", lambda e, kb=kb, i=i, pbank=pbank,
                                      sl0=sl0, xo=xo: e.matmul(
                            out=ps[:, 512 * pbank + 128 * i:512 * pbank + 128 * (i + 1)],
                            lhsT=xn_bf[0:128, sl0 + 128 * kb:sl0 + 128 * (kb + 1)],
                            rhs=ks["ident"][:, :], start=True, stop=True),
                            waits=(bw + [t_ap, t_ln2]) if i == 0 else [],
                            inc=(i == 3))
                    t_tpm = ("m", "T", sch.m["T"])
                    ev = sch.op("V", lambda e, g=g, t=t, pbank=pbank: e.tensor_copy(
                        out=hnT3[:, 4 * g:4 * g + 4, 128 * t:128 * (t + 1)],
                        in_=bank(pbank)[:, 0:512]), waits=[t_tpm], inc=True)
                    sch.bank_read([pbank], ev)
                h_slot_tok[t % 2] = ("m", "T", sch.m["T"])
            t_hnT = ("m", "V", sch.m["V"])

            bwg = sch.bank_waits([6, 7])
            for kb in range(32):
                for piece in range(2):
                    sch.op("T", lambda e, kb=kb, piece=piece: e.matmul(
                        out=bank(6 + piece)[0:64, :],
                        lhsT=gza_bf[:, 64 * kb:64 * (kb + 1)],
                        rhs=hnT3[:, kb, 512 * piece:512 * (piece + 1)],
                        start=(kb == 0), stop=False),
                        waits=(bwg + [t_gzac, t_hnT]) if kb == 0 and piece == 0 else [])
            for piece in range(2):
                sch.op("T", lambda e, piece=piece: e.matmul(
                    out=bank(6 + piece)[0:64, :], lhsT=eb_bf[:, :],
                    rhs=ks["ones_bf"][0:1, 512 * piece:512 * (piece + 1)],
                    start=False, stop=True), waits=[t_ebc], inc=(piece == 1))
            t_gmm = ("m", "T", sch.m["T"])
            tz1 = sch.op("V", lambda e: e.tensor_copy(
                out=gz_sb[:, 0:512], in_=bank(6)[0:64, :]), waits=[t_gmm])
            tz2 = sch.op("V", lambda e: e.tensor_copy(
                out=gz_sb[:, 512:1024], in_=bank(7)[0:64, :]), waits=[t_gmm],
                inc=True)
            sch.bank_read([6, 7], tz2)
            bw6 = sch.bank_waits([6])
            for sub in range(8):
                sch.op("T", lambda e, sub=sub: e.matmul(
                    out=bank(6)[:, 4 * sub:4 * (sub + 1)],
                    lhsT=gz_sb[0:4, 128 * sub:128 * (sub + 1)],
                    rhs=ks["ident4"][:, :], start=True, stop=True),
                    waits=(bw6 + [tz2]) if sub == 0 else [], inc=(sub == 7))
            t_lt = ("m", "T", sch.m["T"])
            mx_toks = []
            for sub in range(8):
                tk = sch.op("V", lambda e, sub=sub: e.tensor_reduce(
                    out=stt[:, 16 + 0:16 + 1] if False else oh_tok[:, 0:1].bitcast(BF16) if False else stt[:, 16:17],
                    in_=bank(6)[:, 4 * sub:4 * (sub + 1)], axis=AX.X, op=ALU.max),
                    waits=[t_lt] + (mx_toks[-1:]), inc=True)
                tk = sch.op("V", lambda e, sub=sub: e.tensor_scalar(
                    out=oh_tok[:, 4 * sub:4 * (sub + 1)],
                    in0=bank(6)[:, 4 * sub:4 * (sub + 1)],
                    scalar1=stt[:, 16:17], scalar2=None, op0=ALU.is_ge),
                    waits=[tk], inc=True)
                mx_toks.append(tk)
            t_oh = mx_toks[-1]
            sch.bank_read([6], t_oh)
            bw67 = sch.bank_waits([6, 7])
            for sub in range(8):
                sch.op("T", lambda e, sub=sub: e.matmul(
                    out=bank(6 + sub // 4)[0:4, 128 * (sub % 4):128 * (sub % 4 + 1)],
                    lhsT=oh_tok[:, 4 * sub:4 * (sub + 1)],
                    rhs=ks["ident"][:, :], start=True, stop=True),
                    waits=(bw67 + [t_oh]) if sub == 0 else [], inc=(sub == 7))
            t_oht = ("m", "T", sch.m["T"])
            toe = sch.op("V", lambda e: e.tensor_copy(
                out=ohT_sb[:, 0:512], in_=bank(6)[0:4, :]), waits=[t_oht])
            toe = sch.op("V", lambda e: e.tensor_copy(
                out=ohT_sb[:, 512:1024], in_=bank(7)[0:4, :]), inc=True)
            sch.bank_read([6, 7], toe)
            bw6b = sch.bank_waits([6, 7])
            for piece in range(2):
                sch.op("T", lambda e, piece=piece: e.matmul(
                    out=bank(6 + piece)[32:64, :], lhsT=ks["repmat"][:, :],
                    rhs=ohT_sb[:, 512 * piece:512 * (piece + 1)],
                    start=True, stop=True, tile_position=(0, 32)),
                    waits=(bw6b + [toe]) if piece == 0 else [], inc=(piece == 1))
            t_rep = ("m", "T", sch.m["T"])
            tzs = None
            for piece in range(2):
                tzs = sch.op("V", lambda e, piece=piece: e.tensor_tensor(
                    out=zsel[32:64, 512 * piece:512 * (piece + 1)],
                    in0=gz_sb[32:64, 512 * piece:512 * (piece + 1)],
                    in1=bank(6 + piece)[32:64, :], op=ALU.mult),
                    waits=[t_rep], inc=True)
            t_zsel = tzs
            sch.bank_read([6, 7], t_zsel)

            fev = {}
            fstat = {}
            for ob in range(32):
                i = wseq[0]
                t_cast = wchunk(i, D,
                                wup_d[ob].rearrange("(kb p) c -> p kb c", p=128),
                                ("m", "T", sch.m["T"]), dst3=(32, 128))
                wseq[0] += 1
                pl = (ob % 2) * 2
                bw = sch.bank_waits([pl, pl + 1])
                for kb in range(32):
                    for piece in range(2):
                        sch.op("T", lambda e, i=i, kb=kb, piece=piece, pl=pl: e.matmul(
                            out=bank(pl + piece)[:, :],
                            lhsT=wb[:, (i % 2) * D + 128 * kb:(i % 2) * D + 128 * (kb + 1)],
                            rhs=hnT3[:, kb, 512 * piece:512 * (piece + 1)],
                            start=(kb == 0), stop=False),
                            waits=(bw + [t_cast, t_hnT]) if kb == 0 and piece == 0 else [])
                for piece in range(2):
                    sch.op("T", lambda e, ob=ob, piece=piece, pl=pl: e.matmul(
                        out=bank(pl + piece)[:, :],
                        lhsT=bst_bf[32:64, 128 * ob:128 * (ob + 1)],
                        rhs=zsel[32:64, 512 * piece:512 * (piece + 1)],
                        start=False, stop=True),
                        waits=[t_zsel, t_bstc] if piece == 0 else [],
                        inc=(piece == 1))
                t_mm = ("m", "T", sch.m["T"])
                fsl = (ob % 2) * 1024
                t_ev = sch.op("V", lambda e, pl=pl, fsl=fsl: e.tensor_copy(
                    out=fbuf[:, fsl:fsl + 1024],
                    in_=ps[:, 512 * pl:512 * pl + 1024]),
                    waits=[t_mm, t_zsel] + ([fev[ob - 2]] if ob - 2 in fev else []),
                    inc=True)
                sch.bank_read([pl, pl + 1], t_ev)
                t_sq = sch.op("S", lambda e, fsl=fsl: e.activation(
                    out=fsq[:, :], in_=fbuf[:, fsl:fsl + 1024], func=AF.Square),
                    waits=[t_ev] + ([fstat[ob - 1]] if ob - 1 in fstat else []),
                    inc=True)
                bw2 = sch.bank_waits([4, 5, 6, 7]) if ob == 0 else []
                for piece in range(2):
                    sch.op("T", lambda e, piece=piece, fsl=fsl, ob=ob: e.matmul(
                        out=bank(4 + piece)[0:1, :], lhsT=ks["ones_bf"][:, 0:1],
                        rhs=fbuf[:, fsl + 512 * piece:fsl + 512 * (piece + 1)],
                        start=(ob == 0), stop=(ob == 31)),
                        waits=(bw2 + [t_ev]) if piece == 0 else [])
                for piece in range(2):
                    sch.op("T", lambda e, piece=piece, ob=ob: e.matmul(
                        out=bank(6 + piece)[0:1, :], lhsT=ks["ones_bf"][:, 0:1],
                        rhs=fsq[:, 512 * piece:512 * (piece + 1)],
                        start=(ob == 0), stop=(ob == 31)),
                        waits=[t_sq], inc=(piece == 1))
                fstat[ob] = ("m", "T", sch.m["T"])
                fev[ob] = sch.dma("G", fyd[128 * ob:128 * (ob + 1), :],
                                  fbuf[:, fsl:fsl + 1024], waits=[t_ev],
                                  stream=f"sf{ob % 2}")

            t_ln4 = row_stats([4, 5], [6, 7], frow, 0, fstat[31])
            cmb4 = combine_pass(fyd, 0, x2d, 0, xout, t_ln4, "rf", "r2", "xo",
                                {0: fev[30], 1: fev[31]})
            if mode in ("C", "full"):
                for s_ in range(2):
                    sch.wait("Y", ("d", f"xo{s_}", sch.d[f"xo{s_}"]))

        blk = ctx.enter_context(nc.Block())

        @blk.tensor
        def _(e):
            for fn in sch.q["T"]:
                fn(e)

        @blk.vector
        def _(e):
            for fn in sch.q["V"]:
                fn(e)

        @blk.scalar
        def _(e):
            for fn in sch.q["S"]:
                fn(e)

        @blk.gpsimd
        def _(e):
            for fn in sch.q["G"]:
                fn(e)

        @blk.sync
        def _(e):
            for fn in sch.q["Y"]:
                fn(e)

    return nc


_BUILD_CACHE = {}


def _get_build(mode):
    if mode not in _BUILD_CACHE:
        _BUILD_CACHE[mode] = build(mode)
    return _BUILD_CACHE[mode]


def kernel(**inputs):
    cores = list(range(8))
    sh = prep_shared(inputs)
    pc = [prep_core(inputs, c) for c in cores]
    drop_a = ("wo_ob", "wup_ob", "gza", "ebias", "bstack")

    ncA = _get_build("A")
    in_maps = []
    for c in cores:
        m = dict(sh)
        m.update(pc[c])
        for k in drop_a:
            m.pop(k)
        in_maps.append(m)
    resA = run_bass_kernel_spmd(ncA, in_maps, core_ids=cores)
    agout = np.concatenate(
        [np.asarray(resA.results[c]["contrib"]) for c in cores], axis=0)

    ncB = _get_build("B")
    in_maps = []
    for c in cores:
        m = dict(sh)
        m.update(pc[c])
        for k in drop_a:
            m.pop(k)
        m["agout"] = agout
        in_maps.append(m)
    resB = run_bass_kernel_spmd(ncB, in_maps, core_ids=cores)
    ag2 = np.concatenate(
        [np.asarray(resB.results[c]["attn_out"]) for c in cores], axis=0)

    ncC = _get_build("C")
    in_maps = []
    for c in cores:
        m = dict(sh)
        m.update(pc[c])
        m["ag2"] = ag2
        in_maps.append(m)
    resC = run_bass_kernel_spmd(ncC, in_maps, core_ids=cores)

    out = np.empty((NB, S, D), np.float32)
    for c in cores:
        b, j = c // 4, c % 4
        out[b, j * CH:(j + 1) * CH] = np.asarray(resC.results[c]["xout"])
    return out


# version 04 (final)
# speedup vs baseline: 1.1067x; 1.1067x over previous
from contextlib import ExitStack
import numpy as np
import ml_dtypes

import concourse.bass as bass
import concourse.mybir as mybir
from concourse.bass_utils import run_bass_kernel_spmd
from concourse.libnrt import physical_to_logical_core_id as _p2l

F32 = mybir.dt.float32
BF16 = mybir.dt.bfloat16
AF = mybir.ActivationFunctionType
ALU = mybir.AluOpType
AX = mybir.AxisListType
BF = ml_dtypes.bfloat16

NB, S, D = 2, 4096, 4096
CH = 1024
HALO = 4
TW = CH + HALO
NQH, NKVH, HD = 16, 4, 64
LAT, KVD = 1024, 256
EPS = 1e-5
CC_DELAY_NOPS = 4

ENGS = ("T", "V", "S", "G", "Y")



def _mk_perm(n_heads):
    perm = np.zeros(n_heads * HD, dtype=np.int64)
    for h in range(n_heads):
        blk, a = h // 2, h % 2
        for i in range(HD):
            pos = (a * 32 + i // 2) if i % 2 == 0 else (64 + a * 32 + (i - 1) // 2)
            perm[blk * 128 + pos] = h * HD + i
    return perm


PQ = np.arange(NQH * HD)
PK = np.arange(NKVH * HD)


def _conv_block_weights(w, perm, n_blk):
    out = np.zeros((n_blk, 3, 128, 128), np.float32)
    for blk in range(n_blk):
        o = perm[blk * 128:(blk + 1) * 128]
        for pi in range(128):
            same = (o // HD) == (o[pi] // HD)
            idx = np.nonzero(same)[0]
            out[blk, :, pi, idx] = w[o[idx], o[pi] % HD, :]
    return out


def _dense_conv_weights(w, perm, n_ob):
    wp = w[np.ix_(perm, perm)]
    out = np.empty((n_ob, w.shape[1], 3, 128), np.float32)
    for ob in range(n_ob):
        out[ob] = wp[ob * 128:(ob + 1) * 128].transpose(1, 2, 0)
    return np.ascontiguousarray(out)


def prep_shared(a):
    sh = {}
    wq = np.asarray(a["L0_attn_w_qkv"], np.float32)
    wq_p = np.concatenate([wq[:LAT][PQ], wq[LAT:LAT + KVD][PK], wq[LAT + KVD:]], 0)
    wqkv_t = np.ascontiguousarray(wq_p.T)
    sh["wqkv_ob"] = np.ascontiguousarray(wqkv_t.reshape(D, 12, 128).transpose(1, 0, 2))
    sh["c1q"] = _conv_block_weights(np.asarray(a["L0_attn_qc1_w"], np.float32), PQ, 8)
    sh["c1k"] = _conv_block_weights(np.asarray(a["L0_attn_kc1_w"], np.float32), PK, 2)
    sh["c2q"] = _dense_conv_weights(np.asarray(a["L0_attn_qc2_w"], np.float32), PQ, 8)
    sh["c2k"] = _dense_conv_weights(np.asarray(a["L0_attn_kc2_w"], np.float32), PK, 2)
    cb = np.zeros((128, 20), np.float32)
    cb[:, 0:8] = np.asarray(a["L0_attn_qc1_b"], np.float32)[PQ].reshape(8, 128).T
    cb[:, 8:16] = np.asarray(a["L0_attn_qc2_b"], np.float32)[PQ].reshape(8, 128).T
    cb[:, 16:18] = np.asarray(a["L0_attn_kc1_b"], np.float32)[PK].reshape(2, 128).T
    cb[:, 18:20] = np.asarray(a["L0_attn_kc2_b"], np.float32)[PK].reshape(2, 128).T
    sh["cb"] = cb
    wo = np.asarray(a["L0_attn_w_o"], np.float32)
    sh["wo_ob"] = np.ascontiguousarray(wo.T.reshape(LAT, 32, 128).transpose(1, 0, 2))
    wup = np.asarray(a["L0_ffn_W_up"], np.float32)
    sh["wup_ob"] = np.ascontiguousarray(wup.T.reshape(D, 32, 128).transpose(1, 0, 2))
    gza = np.zeros((D, 64), np.float32)
    gza[:, 0:4] = np.asarray(a["L0_ffn_gate_w"], np.float32).T
    gza[:, 32:64] = np.asarray(a["L0_ffn_lora_A"], np.float32).reshape(32, D).T
    sh["gza"] = gza
    eb = np.zeros((1, 64), np.float32)
    eb[0, 0:4] = np.asarray(a["L0_ffn_expert_bias"], np.float32)
    sh["ebias"] = eb
    sh["bstack"] = np.ascontiguousarray(
        np.asarray(a["L0_ffn_lora_B"], np.float32).transpose(0, 2, 1).reshape(32, D))
    sh["ktemp"] = np.ascontiguousarray(np.broadcast_to(
        np.asarray(a["L0_attn_key_temp"], np.float32).reshape(1, 1), (128, 1)))
    return sh


def prep_core(a, core):
    b, j = core // 4, core % 4
    x = np.asarray(a["L0_attn_x"], np.float32)
    t0 = j * CH
    xs = np.zeros((TW, D), np.float32)
    lo = max(0, t0 - HALO)
    xs[HALO - (t0 - lo):] = x[b, lo:t0 + CH]
    inv = 1.0 / 10000.0 ** (np.arange(0, HD, 2, dtype=np.float64) / HD)
    pos = np.arange(t0, t0 + CH, dtype=np.float64)
    ang = (pos[:, None] * inv[None, :]).reshape(8, 128, 32)
    cs = np.cos(ang).transpose(1, 0, 2).reshape(128, 256).astype(np.float32)
    sn = np.sin(ang).transpose(1, 0, 2).reshape(128, 256).astype(np.float32)
    return {
        "xs": xs,
        "cosT": np.ascontiguousarray(cs),
        "sinT": np.ascontiguousarray(sn),
        "hmask": np.full((128, 1), 0.0 if j == 0 else 1.0, np.float32),
    }


def _consts():
    c = {}
    c["ident"] = np.eye(128, dtype=BF)
    p = np.arange(128)
    rowmap = p % 64
    P = np.zeros((128, 64), BF)
    P[p, rowmap] = 1
    c["meanpat"] = P
    kmbc = np.zeros((64, 128), BF)
    kmbc[rowmap, p] = 0.125
    c["kmbc"] = kmbc
    qmbc = np.zeros((64, 128), BF)
    qmbc[rowmap, p] = 0.03125
    c["qmbc"] = qmbc
    L = np.zeros((128, 2), BF)
    L[p < 64, 0] = 1
    L[p >= 64, 1] = 1
    c["l2pat"] = L
    BCm = np.zeros((2, 128), BF)
    BCm[p // 64, p] = 1
    c["l2bc"] = BCm
    ii, jj = np.meshgrid(np.arange(128), np.arange(128), indexing="ij")
    c["stair"] = (jj >= ii).astype(BF)
    Dm = np.zeros((128, 256), BF)
    Dm[32 * (p // 64), p] = 1
    Dm[64 + 32 * (p // 64), 128 + p] = 1
    c["denbc"] = Dm.astype(np.float32)
    c["zeros_bf"] = np.zeros((128, 1024), BF)
    e64 = np.zeros((128, 64), np.float32)
    e64[64, :] = 1
    c["e64"] = e64
    Rep = np.zeros((4, 32), BF)
    Rep[np.arange(32) // 8, np.arange(32)] = 1
    c["repmat"] = Rep
    c["ones_bf"] = np.ones((128, 1024), BF)
    c["ident4"] = np.eye(4, dtype=np.float32)
    fc = np.zeros((128, 4), np.float32)
    fc[:, 0] = EPS
    fc[:, 1] = 1.0
    c["fconst"] = fc
    return c



def _dyn2(base, row, nrow, col, ncol, p):
    ap = base[bass.ds(row, nrow), bass.ds(col, ncol)]
    return ap.rearrange("(t p) c -> p t c", p=p)


def _dynkv(base, row, col):
    ap = base[bass.ds(row, 256), :]
    ap = ap.rearrange("r c -> (r c)").rearrange("(t v) -> t v", v=256)
    ap = ap[:, bass.ds(col, 64)]
    return ap.rearrange("(t p) v -> p t v", p=128)



class Sched:
    def __init__(self, nc, ctx):
        self.nc = nc
        self.ctx = ctx
        self.q = {e: [] for e in ENGS}
        self.m = {e: 0 for e in ENGS}
        self.d = {}
        self.sem_m = {e: ctx.enter_context(nc.semaphore(f"m_{e}")) for e in ENGS}
        self.sem_d = {}
        self.waited = {e: {} for e in ENGS}
        self.banklast = {}

    def wait(self, eng, tok):
        if tok is None:
            return
        kind, pe, cnt = tok
        key = (kind, pe)
        if self.waited[eng].get(key, -1) >= cnt:
            return
        self.waited[eng][key] = cnt
        sem = self.sem_m[pe] if kind == "m" else self.sem_d[pe]
        val = cnt if kind == "m" else 16 * cnt
        self.q[eng].append(lambda e, sem=sem, val=val: e.wait_ge(sem, val))

    def _dsem(self, stream):
        if stream not in self.sem_d:
            self.sem_d[stream] = self.ctx.enter_context(
                self.nc.semaphore(f"d_{stream}"))
            self.d[stream] = 0
        return self.sem_d[stream]

    def op(self, eng, fn, waits=(), inc=False):
        for t in waits:
            self.wait(eng, t)
        if inc:
            self.m[eng] += 1
            sem = self.sem_m[eng]
            cnt = self.m[eng]
            self.q[eng].append(lambda e, fn=fn, sem=sem: fn(e).then_inc(sem, 1))
            self.q[eng].append(lambda e, sem=sem, cnt=cnt: e.wait_ge(sem, cnt))
            return ("m", eng, cnt)
        self.q[eng].append(fn)
        return None

    def dma(self, eng, out, in_, waits=(), stream=None, nobc=False):
        stream = stream or eng
        sem = self._dsem(stream)
        for t in waits:
            self.wait(eng, t)
        self.d[stream] += 1

        idx = self.d[stream]

        def run(e, out=out, in_=in_, sem=sem, stream=stream, idx=idx, nobc=nobc):
            if callable(out):
                out = out()
            if callable(in_):
                in_ = in_()
            kw = {"bounds_check": None} if nobc else {}
            try:
                e.dma_start(out=out, in_=in_, **kw).then_inc(sem, 16)
            except Exception:
                print(f"DMA FAIL stream={stream} idx={idx}")
                raise
        self.q[eng].append(run)
        return ("d", stream, self.d[stream])

    def raw(self, eng, fn):
        self.q[eng].append(fn)

    def dma_switch(self, eng, pairs_fn, n_dma, waits=(), stream=None):
        stream = stream or eng
        sem = self._dsem(stream)
        for t in waits:
            self.wait(eng, t)
        self.d[stream] += n_dma

        def run(e, sem=sem, pairs_fn=pairs_fn, n_dma=n_dma):
            for arm in e.switch_core_id(n=128):
                r = _p2l(int(arm), 0, container_num_cores=8) % 8
                pairs = pairs_fn(r)
                assert len(pairs) == n_dma
                for (o, i) in pairs:
                    e.dma_start(out=o, in_=i).then_inc(sem, 16)
        self.q[eng].append(run)
        return ("d", stream, self.d[stream])

    def local_barrier(self):
        toks = {}
        for e in ENGS:
            toks[e] = self.mark(e)
        for e in ENGS:
            for pe in ENGS:
                if pe != e:
                    self.wait(e, toks[pe])

    def mark(self, eng):
        self.m[eng] += 1
        sem = self.sem_m[eng]
        cnt = self.m[eng]
        self.q[eng].append(lambda e, sem=sem: e.nop().then_inc(sem, 1))
        self.q[eng].append(lambda e, sem=sem, cnt=cnt: e.wait_ge(sem, cnt))
        return ("m", eng, cnt)

    def bank_waits(self, banks):
        out = []
        for b in banks:
            t = self.banklast.get(b)
            if t is not None:
                out.append(t)
        return out

    def bank_read(self, banks, tok):
        for b in banks:
            self.banklast[b] = tok



def build(mode="full", dbg=False):
    nc = bass.Bass("TRN2", target_bir_lowering=False, debug=False)
    C = _consts()

    def din(name, shape, dt=F32):
        return nc.dram_tensor(name, shape, dt, kind="ExternalInput").ap()

    xs = din("xs", [TW, D])
    wqkv = din("wqkv_ob", [12, D, 128])
    c1q_d = din("c1q", [8, 3, 128, 128])
    c1k_d = din("c1k", [2, 3, 128, 128])
    c2q_d = din("c2q", [8, LAT, 3, 128])
    c2k_d = din("c2k", [2, KVD, 3, 128])
    cb_d = din("cb", [128, 20])
    cos_d = din("cosT", [128, 256])
    sin_d = din("sinT", [128, 256])
    hm_d = din("hmask", [128, 1])
    kt_d = din("ktemp", [128, 1])

    cst = {k: nc.inline_tensor(np.ascontiguousarray(v), name="c_" + k).ap()
           for k, v in C.items()}

    if mode == "A":
        contrib = nc.dram_tensor("contrib", [1536, CH], BF16, kind="ExternalOutput").ap()
    elif mode == "full":
        contrib = nc.dram_tensor("cc1_in", [1536, CH], BF16, kind="Internal").ap()
    else:
        contrib = None
    if mode == "B":
        agout = nc.dram_tensor("agout", [12288, CH], BF16, kind="ExternalInput").ap()
        attn_out = nc.dram_tensor("attn_out", [256, S], BF16,
                                  kind="ExternalOutput").ap()
    elif mode == "full":
        agout = nc.dram_tensor("cc1_out", [12288, CH], BF16, kind="Internal",
                               addr_space="Shared").ap()
        attn_out = nc.dram_tensor("cc2_in", [256, S], BF16, kind="Internal").ap()
    if mode == "C":
        ag2 = nc.dram_tensor("ag2", [2048, S], BF16, kind="ExternalInput").ap()
    elif mode == "full":
        ag2 = nc.dram_tensor("cc2_out", [2048, S], BF16, kind="Internal",
                             addr_space="Shared").ap()
    if mode == "full":
        fl_in = nc.dram_tensor("fl_in", [16, 128], BF16, kind="Internal").ap()
        fl_out1 = nc.dram_tensor("fl_out1", [128, 128], BF16,
                                 kind="Internal").ap()
        fl_out2 = nc.dram_tensor("fl_out2", [128, 128], BF16,
                                 kind="Internal").ap()
    if mode in ("C", "full"):
        xout = nc.dram_tensor("xout", [CH, D], F32, kind="ExternalOutput").ap()
        wo_d = nc.dram_tensor("wo_ob", [32, LAT, 128], F32, kind="ExternalInput").ap()
        wup_d = nc.dram_tensor("wup_ob", [32, D, 128], F32, kind="ExternalInput").ap()
        gza_d = nc.dram_tensor("gza", [D, 64], F32, kind="ExternalInput").ap()
        eb_d = nc.dram_tensor("ebias", [1, 64], F32, kind="ExternalInput").ap()
        bst_d = nc.dram_tensor("bstack", [32, D], F32, kind="ExternalInput").ap()
        x2d = nc.dram_tensor("x2d", [CH, D], F32, kind="Internal").ap()
        wyd = nc.dram_tensor("wyd", [D, CH], BF16, kind="Internal").ap()
        fyd = nc.dram_tensor("fyd", [D, CH], BF16, kind="Internal").ap()

    ctx = ExitStack()
    with ctx:
        sbt = lambda name, shape, dt: ctx.enter_context(nc.sbuf_tensor(name, shape, dt))
        x2t = sbt("x2t", [128, 2 * D], F32)
        xn_bf = sbt("xn_bf", [128, 2 * D], BF16)
        xnT = sbt("xnT", [128, 32 * 1032], BF16)
        wf = sbt("wf", [128, 2 * D], F32)
        wb = sbt("wb", [128, 2 * D], BF16)
        qT = sbt("qT", [128, 8 * TW], BF16)
        kT = sbt("kT", [128, 2 * TW], BF16)
        vT = sbt("vT", [128, 2 * TW], BF16)
        c1wq = sbt("c1wq", [128, 8 * 3 * 128], BF16)
        c1wk = sbt("c1wk", [128, 2 * 3 * 128], BF16)
        cbs = sbt("cbs", [128, 20], F32)
        coss = sbt("coss", [128, 256], F32)
        sins = sbt("sins", [128, 256], F32)
        hms = sbt("hms", [128, 1], F32)
        kts = sbt("kts", [128, 1], F32)
        stt = sbt("stt", [128, 44], F32)
        vpc = sbt("vpc", [128, 2 * 128], BF16)
        ks = {}
        for k in C:
            dt = BF16 if C[k].dtype == BF else F32
            ks[k] = sbt("k_" + k, list(C[k].shape), dt)

        c1q_s = xnT[:, 0:8 * 1026]
        c1k_s = xnT[:, 8 * 1026:10 * 1026]
        off = 10 * 1026 + 2
        qa = xnT[:, off:off + 4096].bitcast(F32)
        off += 4096
        q2 = xnT[:, off:off + 1024]
        off += 1024
        ropet = xnT[:, off:off + 4096].bitcast(F32)
        off += 4096
        qout = xnT[:, off:off + 2048]
        off += 2048
        mean_sb = xnT[0:64, off:off + 2048]
        off += 2048
        kmarr = xnT[:, off:off + 2048].bitcast(F32)
        off += 2048
        qmarr = xnT[:, off:off + 2048].bitcast(F32)
        off += 2048
        rr = xnT[0:2, off:off + 4096].bitcast(F32)
        off += 4096
        rbf = xnT[0:2, off:off + 1024]
        off += 1024
        assert off <= 32 * 1032, off

        ps = ctx.enter_context(nc.psum_tensor("ps", [128, 4096], F32))

        def bank(i):
            return ps[:, 512 * i:512 * (i + 1)]

        xnT3 = xnT[:, :].rearrange("p (kb c) -> p kb c", c=1032)

        sch = Sched(nc, ctx)

        for k in ks:
            idx = tuple(slice(None) for _ in C[k].shape)
            sch.dma("Y", ks[k][idx], cst[k][idx], stream="c")
        for srcd, dst in ((cb_d, cbs), (cos_d, coss), (sin_d, sins),
                          (hm_d, hms), (kt_d, kts)):
            sch.dma("Y", dst[tuple(slice(None) for _ in dst.shape)],
                    srcd[:, :], stream="c")
        sch.dma("Y", wf[:, 0:3072].rearrange("p (b s m) -> p b s m", b=8, s=3),
                c1q_d.rearrange("b s p m -> p b s m"), stream="c")
        t_c1fk = sch.dma("Y", wf[:, 3072:3840].rearrange("p (b s m) -> p b s m", b=2, s=3),
                         c1k_d.rearrange("b s p m -> p b s m"), stream="c")
        t_constd = t_c1fk
        sch.op("G", lambda e: e.tensor_copy(out=c1wq[:, :], in_=wf[:, 0:3072]),
               waits=[t_constd])
        t_c1w = sch.op("G", lambda e: e.tensor_copy(out=c1wk[:, :], in_=wf[:, 3072:3840]),
                       inc=True)
        if mode == "full":
            sch.dma("G", fl_in, cst["zeros_bf"][0:16, 0:128], stream="fl")

        cast_toks = {}
        wdma_toks = {}
        wseq = [0]
        stat_tok = {}

        def wchunk(i, width, src_ap, wb_free_tok, dst3=None):
            dst = wf[:, (i % 2) * D:(i % 2) * D + width]
            if dst3 is not None:
                dst = dst.rearrange("p (a c) -> p a c", a=dst3[0])
            wdma_toks[i] = sch.dma(
                "Y", dst, src_ap,
                waits=([cast_toks[i - 2]] if i - 2 in cast_toks else [t_c1w]),
                stream=f"w{i % 2}")
            w = [wdma_toks[i]]
            if wb_free_tok is not None:
                w.append(wb_free_tok)
            cast_toks[i] = sch.op("G", lambda e, i=i, width=width: e.tensor_copy(
                out=wb[:, (i % 2) * D:(i % 2) * D + width],
                in_=wf[:, (i % 2) * D:(i % 2) * D + width]), waits=w, inc=True)
            return cast_toks[i]


        if mode in ("A", "full"):
            apply_toks = {}
            x_toks = {-1: sch.dma("Y", x2t[0:4, 0:D], xs[0:4, :], stream="x0")}

            def ln_tile(t):
                nrow = 4 if t < 0 else 128
                sl0 = ((t % 2) if t >= 0 else 0) * D
                xt = x2t[0:nrow, sl0:sl0 + D]
                xo = xn_bf[0:nrow, sl0:sl0 + D]
                L = 16 if t < 0 else 8 * (t % 2)
                slot_free = xn_slot_tok.get(t % 2 if t >= 0 else 0)
                t_sq = sch.op("S", lambda e: e.activation(
                    out=xo,
                    in_=xt, func=AF.Square, accum_out=stt[0:nrow, L + 1:L + 2]),
                    waits=[x_toks[t], t_constd] + ([slot_free] if slot_free else []),
                    inc=True)
                t_red = sch.op("V", lambda e: e.tensor_reduce(
                    out=stt[0:nrow, L:L + 1], in_=xt, axis=AX.X, op=ALU.add),
                    waits=[x_toks[t], t_constd], inc=True)
                t_m = sch.op("V", lambda e: e.tensor_scalar(
                    out=stt[0:nrow, L + 2:L + 3], in0=stt[0:nrow, L:L + 1],
                    scalar1=1.0 / D, scalar2=None, op0=ALU.mult),
                    waits=[t_red], inc=True)
                t_msq = sch.op("V", lambda e: e.tensor_tensor(
                    out=stt[0:nrow, L + 3:L + 4], in0=stt[0:nrow, L + 2:L + 3],
                    in1=stt[0:nrow, L + 2:L + 3], op=ALU.mult),
                    waits=[t_m], inc=True)
                t_var = sch.op("V", lambda e: e.tensor_scalar(
                    out=stt[0:nrow, L + 4:L + 5], in0=stt[0:nrow, L + 1:L + 2],
                    scalar1=1.0 / D, scalar2=stt[0:nrow, L + 3:L + 4],
                    op0=ALU.mult, op1=ALU.subtract), waits=[t_sq, t_msq], inc=True)
                t_ln = sch.op("S", lambda e: e.activation(
                    out=stt[0:nrow, L + 5:L + 6], in_=stt[0:nrow, L + 4:L + 5],
                    func=AF.Ln, bias=ks["fconst"][0:nrow, 0:1]), waits=[t_var],
                    inc=True)
                t_rs = sch.op("S", lambda e: e.activation(
                    out=stt[0:nrow, L + 6:L + 7], in_=stt[0:nrow, L + 5:L + 6],
                    func=AF.Exp, scale=-0.5), waits=[t_ln], inc=True)
                t_mn = sch.op("V", lambda e: e.tensor_scalar(
                    out=stt[0:nrow, L + 7:L + 8], in0=stt[0:nrow, L + 2:L + 3],
                    scalar1=-1.0, scalar2=None, op0=ALU.mult), waits=[t_m], inc=True)
                tok = sch.op("V", lambda e: e.tensor_scalar(
                    out=xo, in0=xt, scalar1=stt[0:nrow, L + 7:L + 8],
                    scalar2=stt[0:nrow, L + 6:L + 7], op0=ALU.add, op1=ALU.mult),
                    waits=[t_rs, t_mn] + ([slot_free] if slot_free else []), inc=True)
                apply_toks[t] = tok
                if t < 0:
                    bw = sch.bank_waits([6])
                    for kb in range(32):
                        t_tp = sch.op("T", lambda e, kb=kb: e.matmul(
                            out=ps[:, 512 * 6 + 4 * kb:512 * 6 + 4 * kb + 4],
                            lhsT=xn_bf[0:4, 128 * kb:128 * (kb + 1)],
                            rhs=ks["ident"][0:4, 0:4],
                            start=True, stop=True),
                            waits=(bw + [tok, t_constd]) if kb == 0 else [],
                            inc=(kb == 31))
                    ev = sch.op("V", lambda e: e.tensor_copy(
                        out=xnT3[:, 0:32, 0:4], in_=bank(6)[:, 0:128]),
                        waits=[t_tp], inc=True)
                    sch.bank_read([6], ev)
                else:
                    for g in range(8):
                        pbank = 6 + (g % 2)
                        bw = sch.bank_waits([pbank])
                        for i in range(4):
                            kb = 4 * g + i
                            t_tp = sch.op("T", lambda e, kb=kb, i=i, pbank=pbank: e.matmul(
                                out=ps[:, 512 * pbank + 128 * i:512 * pbank + 128 * (i + 1)],
                                lhsT=xn_bf[0:128, (t % 2) * D + 128 * kb:(t % 2) * D + 128 * (kb + 1)],
                                rhs=ks["ident"][:, :], start=True, stop=True),
                                waits=(bw + [tok]) if i == 0 else [], inc=(i == 3))
                        ev = sch.op("V", lambda e, g=g, t=t, pbank=pbank: e.tensor_copy(
                            out=xnT3[:, 4 * g:4 * g + 4, 4 + t * 128:4 + (t + 1) * 128],
                            in_=bank(pbank)[:, 0:512]), waits=[t_tp], inc=True)
                        sch.bank_read([pbank], ev)
                xn_slot_tok[t % 2 if t >= 0 else 0] = ("m", "T", sch.m["T"])
                if 0 <= t < 6:
                    x_toks[t + 2] = sch.dma("Y", x2t[:, (t % 2) * D:(t % 2 + 1) * D],
                                            xs[4 + (t + 2) * 128:4 + (t + 3) * 128, :],
                                            waits=[apply_toks[t], t_sq],
                                            stream=f"x{t % 2}")

            xn_slot_tok = {}
            ln_tile(-1)
            x_toks[0] = sch.dma("Y", x2t[:, 0:D], xs[4:132, :],
                                waits=[apply_toks[-1], xn_slot_tok[0]], stream="x0")
            x_toks[1] = sch.dma("Y", x2t[:, D:2 * D], xs[132:260, :], stream="x1")
            for t in range(8):
                ln_tile(t)
            t_xnT = ("m", "V", sch.m["V"])

            qkv_mm = {}
            qkv_evac = {}
            for ob in range(12):
                t_cast = sch_wchunk_qkv = wchunk(
                    ob, D, wqkv[ob].rearrange("(kb p) c -> p kb c", p=128),
                    qkv_mm.get(ob - 2), dst3=(32, 128))
                lo = 0 if ob % 2 == 0 else 3
                bw = sch.bank_waits([lo, lo + 1, lo + 2])
                for kb in range(32):
                    w = []
                    if kb == 0:
                        w = bw + [t_cast, t_xnT]
                    for piece, (o, wd, bk) in enumerate(
                            ((0, 512, lo), (512, 512, lo + 1), (1024, 4, lo + 2))):
                        tk_mm = sch.op("T", lambda e, ob=ob, kb=kb, o=o, wd=wd, bk=bk: e.matmul(
                            out=ps[:, 512 * bk:512 * bk + wd],
                            lhsT=wb[:, (ob % 2) * D + 128 * kb:(ob % 2) * D + 128 * (kb + 1)],
                            rhs=xnT3[:, kb, o:o + wd],
                            start=(kb == 0), stop=(kb == 31)),
                            waits=w if piece == 0 else [],
                            inc=(kb == 31 and piece == 2))
                qkv_mm[ob] = tk_mm
                if ob < 8:
                    dst, base = qT, ob * TW
                elif ob < 10:
                    dst, base = kT, (ob - 8) * TW
                else:
                    dst, base = vT, (ob - 10) * TW
                sch.op("V", lambda e, lo=lo, dst=dst, base=base: e.tensor_copy(
                    out=dst[:, base:base + 512], in_=bank(lo)[:, :]),
                    waits=[qkv_mm[ob]])
                sch.op("V", lambda e, lo=lo, dst=dst, base=base: e.tensor_copy(
                    out=dst[:, base + 512:base + 1024], in_=bank(lo + 1)[:, :]))
                qkv_evac[ob] = sch.op("V", lambda e, lo=lo, dst=dst, base=base: e.tensor_copy(
                    out=dst[:, base + 1024:base + 1028], in_=bank(lo + 2)[:, 0:4]), inc=True)
                sch.bank_read([lo, lo + 1, lo + 2], qkv_evac[ob])

            bw = sch.bank_waits([0, 1])
            for b2 in range(2):
                for piece in range(2):
                    t_km = sch.op("T", lambda e, b2=b2, piece=piece: e.matmul(
                        out=bank(piece)[0:64, :], lhsT=ks["meanpat"][:, :],
                        rhs=kT[:, b2 * TW + 4 + 512 * piece:b2 * TW + 4 + 512 * (piece + 1)],
                        start=(b2 == 0), stop=(b2 == 1)),
                        waits=(bw + [qkv_evac[9], t_constd]) if (b2 == 0 and piece == 0) else [],
                        inc=(b2 == 1 and piece == 1))
            for piece in range(2):
                tk = sch.op("V", lambda e, piece=piece: e.tensor_copy(
                    out=mean_sb[:, 512 * piece:512 * (piece + 1)],
                    in_=bank(piece)[0:64, :]), waits=[t_km], inc=(piece == 1))
            t_km_sb = tk
            sch.bank_read([0, 1], t_km_sb)
            bw = sch.bank_waits([2, 3])
            for qb in range(8):
                for piece in range(2):
                    t_qm = sch.op("T", lambda e, qb=qb, piece=piece: e.matmul(
                        out=bank(2 + piece)[0:64, :], lhsT=ks["meanpat"][:, :],
                        rhs=qT[:, qb * TW + 4 + 512 * piece:qb * TW + 4 + 512 * (piece + 1)],
                        start=(qb == 0), stop=(qb == 7)),
                        waits=(bw + [qkv_evac[7], t_constd]) if (qb == 0 and piece == 0) else [],
                        inc=(qb == 7 and piece == 1))
            for piece in range(2):
                tk = sch.op("V", lambda e, piece=piece: e.tensor_copy(
                    out=mean_sb[:, CH + 512 * piece:CH + 512 * (piece + 1)],
                    in_=bank(2 + piece)[0:64, :]), waits=[t_qm], inc=(piece == 1))
            t_qm_sb = tk
            sch.bank_read([2, 3], t_qm_sb)
            bw = sch.bank_waits([4, 5, 6, 7])
            for piece in range(2):
                sch.op("T", lambda e, piece=piece: e.matmul(
                    out=bank(4 + piece)[:, :], lhsT=ks["kmbc"][:, :],
                    rhs=mean_sb[:, 512 * piece:512 * (piece + 1)],
                    start=True, stop=True),
                    waits=(bw + [t_km_sb]) if piece == 0 else [])
                t_bc = sch.op("T", lambda e, piece=piece: e.matmul(
                    out=bank(6 + piece)[:, :], lhsT=ks["qmbc"][:, :],
                    rhs=mean_sb[:, CH + 512 * piece:CH + 512 * (piece + 1)],
                    start=True, stop=True), waits=[t_qm_sb] if piece == 0 else [],
                    inc=(piece == 1))
            for piece in range(2):
                sch.op("V", lambda e, piece=piece: e.tensor_copy(
                    out=kmarr[:, 512 * piece:512 * (piece + 1)],
                    in_=bank(4 + piece)[:, :]), waits=[t_bc])
                tk = sch.op("V", lambda e, piece=piece: e.tensor_copy(
                    out=qmarr[:, 512 * piece:512 * (piece + 1)],
                    in_=bank(6 + piece)[:, :]), inc=(piece == 1))
            t_marr = tk
            sch.bank_read([4, 5, 6, 7], t_marr)

            c1_done = {}
            for isq, (nblk, srcT, wsb, dst, bcol) in enumerate(
                    ((8, qT, c1wq, c1q_s, 0), (2, kT, c1wk, c1k_s, 16))):
                for b3 in range(nblk):
                    pl = (b3 % 2) * 3
                    bw = sch.bank_waits([pl, pl + 1, pl + 2])
                    for tap in range(3):
                        for o, wd, bk in ((0, 512, pl), (512, 512, pl + 1),
                                          (1024, 2, pl + 2)):
                            w = []
                            if tap == 0 and o == 0:
                                w = bw + [t_c1w, t_marr, qkv_evac[11]]
                            t_mm = sch.op("T", lambda e, b3=b3, tap=tap, o=o, wd=wd, bk=bk,
                                   srcT=srcT, wsb=wsb: e.matmul(
                                out=ps[:, 512 * bk:512 * bk + wd],
                                lhsT=wsb[:, (b3 * 3 + tap) * 128:(b3 * 3 + tap + 1) * 128],
                                rhs=srcT[:, b3 * TW + tap + o:b3 * TW + tap + o + wd],
                                start=(tap == 0), stop=(tap == 2)), waits=w,
                                inc=(tap == 2 and o == 1024))
                    t_ev = sch.op("S", lambda e, b3=b3, pl=pl, dst=dst, bcol=bcol: e.activation(
                        out=dst[:, b3 * 1026:(b3 + 1) * 1026],
                        in_=ps[:, 512 * pl:512 * pl + 1026],
                        func=AF.Identity, bias=cbs[:, bcol + b3:bcol + b3 + 1]),
                        waits=[t_mm], inc=True)
                    sch.bank_read([pl, pl + 1, pl + 2], t_ev)
                    c1_done[(isq, b3)] = sch.op("V", lambda e, b3=b3, dst=dst: e.tensor_scalar(
                        out=dst[:, b3 * 1026:b3 * 1026 + 2],
                        in0=dst[:, b3 * 1026:b3 * 1026 + 2],
                        scalar1=hms[:, 0:1], scalar2=None, op0=ALU.mult),
                        waits=[t_ev], inc=True)
            t_c1_all = c1_done[(1, 1)]

            prev_bc_tok = [None]
            qa_slot_tok = {}
            qout_slot_tok = {}

            def assembly(isq, nblk, c2dram, c1src, dst_row0, marr, bcol, wi_base):
                asm = {}
                for ob in range(nblk):
                    wi = wi_base + ob
                    width = 3072 if isq == 0 else 768
                    t_cast = wchunk(wi, width,
                                    c2dram[ob].rearrange("(cb p) s c -> p cb s c", p=128)
                                    .rearrange("p cb s c -> p cb (s c)"),
                                    ("m", "T", sch.m["T"]), dst3=(8 if isq == 0 else 2, 384))
                    ncb = 8 if isq == 0 else 2
                    pl = (ob % 2) * 2
                    bw = sch.bank_waits([pl, pl + 1])
                    for tc in range(2):
                        for cib in range(ncb):
                            for tap in range(3):
                                w = []
                                if tc == 0 and cib == 0 and tap == 0:
                                    w = bw + [t_cast, t_c1_all]
                                t_c2 = sch.op("T", lambda e, tc=tc, cib=cib, tap=tap, pl=pl,
                                       wi=wi, c1src=c1src: e.matmul(
                                    out=bank(pl + tc)[:, :],
                                    lhsT=wb[:, (wi % 2) * D + (cib * 3 + tap) * 128:
                                            (wi % 2) * D + (cib * 3 + tap + 1) * 128],
                                    rhs=c1src[:, cib * 1026 + tc * 512 + tap:
                                              cib * 1026 + tc * 512 + tap + 512],
                                    start=(cib == 0 and tap == 0),
                                    stop=(cib == ncb - 1 and tap == 2)), waits=w,
                                    inc=(tc == 1 and cib == ncb - 1 and tap == 2))
                    slot = (ob % 2) * 1024
                    qa_free = qa_slot_tok.get(slot)
                    c2b = 8 if isq == 0 else 18
                    t_ev = sch.op("S", lambda e, pl=pl, slot=slot, ob=ob, c2b=c2b: e.activation(
                        out=qa[:, slot:slot + 1024], in_=ps[:, 512 * pl:512 * pl + 1024],
                        func=AF.Identity, bias=cbs[:, c2b + ob:c2b + ob + 1]),
                        waits=[t_c2] + ([qa_free] if qa_free else []),
                        inc=True)
                    sch.bank_read([pl, pl + 1], t_ev)
                    t_ad = sch.op("V", lambda e, slot=slot, marr=marr: e.tensor_tensor(
                        out=qa[:, slot:slot + 1024], in0=qa[:, slot:slot + 1024],
                        in1=marr[:, :], op=ALU.add), waits=[t_ev], inc=True)
                    t_q2 = sch.op("S", lambda e, slot=slot: e.activation(
                        out=q2[:, :], in_=qa[:, slot:slot + 1024], func=AF.Square),
                        waits=[t_ad] + ([prev_bc_tok[0]] if prev_bc_tok[0] else []),
                        inc=True)
                    bw2 = sch.bank_waits([4, 5, 6])
                    for piece in range(2):
                        t_l2 = sch.op("T", lambda e, piece=piece: e.matmul(
                            out=bank(4 + piece)[0:2, :], lhsT=ks["l2pat"][:, :],
                            rhs=q2[:, 512 * piece:512 * (piece + 1)],
                            start=True, stop=True),
                            waits=(bw2 + [t_q2]) if piece == 0 else [],
                            inc=(piece == 1))
                    for piece in range(2):
                        tk = sch.op("V", lambda e, piece=piece: e.tensor_copy(
                            out=rr[:, 512 * piece:512 * (piece + 1)],
                            in_=bank(4 + piece)[0:2, :]), waits=[t_l2], inc=(piece == 1))
                    t_rrc = tk
                    sch.bank_read([4, 5], t_rrc)
                    t_n = sch.op("S", lambda e: e.activation(
                        out=rr[:, CH:2 * CH], in_=rr[:, 0:CH], func=AF.Sqrt),
                        waits=[t_rrc], inc=True)
                    t_mx = sch.op("V", lambda e: e.tensor_scalar(
                        out=rr[:, CH:2 * CH], in0=rr[:, CH:2 * CH],
                        scalar1=1e-12, scalar2=None, op0=ALU.max), waits=[t_n],
                        inc=True)
                    t_rc = sch.op("V", lambda e: e.reciprocal(
                        out=rr[:, 0:CH], in_=rr[:, CH:2 * CH]), waits=[t_mx],
                        inc=True)
                    if isq == 1:
                        t_rc = sch.op("V", lambda e: e.tensor_scalar(
                            out=rr[:, 0:CH], in0=rr[:, 0:CH],
                            scalar1=kts[0:2, 0:1], scalar2=None, op0=ALU.mult),
                            waits=[t_rc], inc=True)
                    t_rbf = sch.op("V", lambda e: e.tensor_copy(
                        out=rbf[:, :], in_=rr[:, 0:CH]), waits=[t_rc], inc=True)
                    bw3 = sch.bank_waits([5, 6])
                    for piece in range(2):
                        t_bcm = sch.op("T", lambda e, piece=piece: e.matmul(
                            out=bank(5 + piece)[:, :], lhsT=ks["l2bc"][:, :],
                            rhs=rbf[:, 512 * piece:512 * (piece + 1)],
                            start=True, stop=True),
                            waits=(bw3 + [t_rbf]) if piece == 0 else [],
                            inc=(piece == 1))
                    prev_bc_tok[0] = t_bcm
                    for piece in range(2):
                        tk = sch.op("V", lambda e, piece=piece, slot=slot: e.tensor_tensor(
                            out=q2[:, 512 * piece:512 * (piece + 1)],
                            in0=qa[:, slot + 512 * piece:slot + 512 * (piece + 1)],
                            in1=bank(5 + piece)[:, :], op=ALU.mult),
                            waits=[t_bcm, t_ad], inc=True)
                    t_qn = tk
                    sch.bank_read([5, 6], t_qn)
                    qa_slot_tok[slot] = t_qn
                    last = None
                    prev_rp = qa_slot_tok.get("rope")
                    for tt in range(8):
                        tb = 4 if tt % 2 == 0 else 7
                        bw4 = sch.bank_waits([tb])
                        tmm = sch.op("T", lambda e, tt=tt, tb=tb: e.matmul(
                            out=bank(tb)[:, 0:128],
                            lhsT=q2[:, tt * 128:(tt + 1) * 128],
                            rhs=ks["ident"][:, :], start=True, stop=True),
                            waits=bw4 + [t_qn], inc=True)
                        pc = bank(tb)[:, 0:128].rearrange("p (h x two) -> p h x two",
                                                          two=2, x=32)
                        qe = pc[:, :, :, 0]
                        qo = pc[:, :, :, 1]
                        def _bc2(apx):
                            return bass.AP(apx.tensor, apx.offset,
                                           [list(apx.ap[0]), [0, 2], list(apx.ap[1])])
                        cA = _bc2(coss[:, tt * 32:(tt + 1) * 32])
                        sA = _bc2(sins[:, tt * 32:(tt + 1) * 32])
                        osl = (ob % 2) * 1024 + tt * 128
                        oc = qout[:, osl:osl + 128].rearrange(
                            "p (h x two) -> p h x two", two=2, x=32)
                        qfree = qout_slot_tok.get(ob % 2)
                        t1 = ropet[:, 0:64].rearrange("p (h x) -> p h x", h=2)
                        t2 = ropet[:, 64:128].rearrange("p (h x) -> p h x", h=2)
                        w1 = sch.op("V", lambda e, qe=qe, cA=cA, t1=t1: e.tensor_tensor(
                            out=t1, in0=qe, in1=cA, op=ALU.mult),
                            waits=[tmm] + ([qfree] if qfree else [])
                            + ([prev_rp] if prev_rp else []), inc=True)
                        w2 = sch.op("V", lambda e, qo=qo, sA=sA, t2=t2: e.tensor_tensor(
                            out=t2, in0=qo, in1=sA, op=ALU.mult), inc=True)
                        w3 = sch.op("V", lambda e, oc=oc, t1=t1, t2=t2: e.tensor_tensor(
                            out=oc[:, :, :, 0], in0=t1, in1=t2, op=ALU.subtract),
                            waits=[w1, w2], inc=True)
                        w4 = sch.op("V", lambda e, qe=qe, sA=sA, t1=t1: e.tensor_tensor(
                            out=t1, in0=qe, in1=sA, op=ALU.mult), waits=[w3], inc=True)
                        w5 = sch.op("V", lambda e, qo=qo, cA=cA, t2=t2: e.tensor_tensor(
                            out=t2, in0=qo, in1=cA, op=ALU.mult), waits=[w3], inc=True)
                        t_rp = sch.op("V", lambda e, oc=oc, t1=t1, t2=t2: e.tensor_tensor(
                            out=oc[:, :, :, 1], in0=t1, in1=t2, op=ALU.add),
                            waits=[w4, w5], inc=True)
                        prev_rp = t_rp
                        qa_slot_tok["rope"] = t_rp
                        sch.bank_read([tb], t_rp)
                        if isq == 0:
                            dst = contrib[tt * 128:(tt + 1) * 128,
                                          ob * 128:(ob + 1) * 128]
                        else:
                            kreg = contrib[1024:1280, :].rearrange(
                                "r c -> (r c)").rearrange("(t v) -> t v", v=256)
                            dst = kreg[tt * 128:(tt + 1) * 128,
                                       ob * 128:(ob + 1) * 128]
                        last = sch.dma("G", dst, qout[:, osl:osl + 128],
                                       waits=[t_rp], stream=f"o{ob % 2}")
                    asm[ob] = last
                    qout_slot_tok[ob % 2] = last
                return asm

            q_asm = assembly(0, 8, c2q_d, c1q_s, 0, kmarr, 0, 12)
            k_asm = assembly(1, 2, c2k_d, c1k_s, 1024, qmarr, 8, 20)
            wseq[0] = 22

            vreg = contrib[1280:1536, :].rearrange("r c -> (r c)").rearrange(
                "(t v) -> t v", v=256)
            v_ev = {}
            for i in range(16):
                vb, t = i // 8, i % 8
                coff = (4 if vb == 0 else 3) + t * 128
                slot = (i % 2) * 128
                bk = 7
                bw = sch.bank_waits([bk])
                tmm = sch.op("T", lambda e, vb=vb, coff=coff, slot=slot, bk=bk: e.matmul(
                    out=ps[:, 512 * bk + slot:512 * bk + slot + 128],
                    lhsT=vT[:, vb * TW + coff:vb * TW + coff + 128],
                    rhs=ks["ident"][:, :], start=True, stop=True),
                    waits=bw + [qkv_evac[11]], inc=True)
                tev = sch.op("V", lambda e, slot=slot, bk=bk: e.tensor_copy(
                    out=vpc[:, slot:slot + 128],
                    in_=ps[:, 512 * bk + slot:512 * bk + slot + 128]),
                    waits=[tmm] + ([v_ev[i - 2]] if i - 2 in v_ev else []), inc=True)
                sch.bank_read([bk], tev)
                v_ev[i] = sch.dma("G", vreg[t * 128:(t + 1) * 128, vb * 128:(vb + 1) * 128],
                                  vpc[:, slot:slot + 128], waits=[tev],
                                  stream=f"v{i % 2}")

            if mode == "A":
                for st in ("o0", "o1", "v0", "v1"):
                    if st in sch.d:
                        sch.wait("Y", ("d", st, sch.d[st]))


        cc1_sem = ctx.enter_context(nc.semaphore("cc1_sem"))
        cc2_sem = ctx.enter_context(nc.semaphore("cc2_sem"))
        if mode == "full":
            for st_ in ("o0", "o1", "v0", "v1"):
                sch.wait("G", ("d", st_, sch.d[st_]))

            sch.local_barrier()

            def _cc1(e):
                e.collective_compute(
                    "AllGather", ALU.bypass,
                    replica_groups=[[0, 1, 2, 3, 4, 5, 6, 7]],
                    ins=[contrib], outs=[agout],
                ).then_inc(cc1_sem)
            sch.raw("G", _cc1)
            sch.wait("G", ("d", "fl", sch.d["fl"]))

            def _fl1(e):
                e.collective_compute(
                    "AllGather", ALU.bypass,
                    replica_groups=[[0, 1, 2, 3, 4, 5, 6, 7]],
                    ins=[fl_in], outs=[fl_out1],
                ).then_inc(cc1_sem)
            sch.raw("G", _fl1)
            for eng in ENGS:
                sch.raw(eng, lambda e: e.wait_ge(cc1_sem, 2))

        if mode in ("B", "full"):
            dsl = bass.ds
            qg = xnT[:, 0:8192]
            kg = xnT[:, 8192:10240]
            vgo = xnT[:, 10240:12320]
            QT01 = xnT[:, 12320:16416]
            QT23 = xnT[:, 16416:20512]
            KT2 = xnT[:, 20512:24608]
            att = xnT[:, 24608:28704]
            rec_f = xnT[:, 28704:32800].bitcast(F32)
            ob_out = xnT[:, 0:2048]
            bc_f = xnT[:, 2048:6144].bitcast(F32)
            def _kv_src(r0, col):
                ap = agout[r0:r0 + 256, :].rearrange("r c -> (r c)")
                ap = ap.rearrange("(t v) -> t v", v=256)[:, col:col + 64]
                return ap.rearrange("(t p) v -> p t v", p=128)

            def _gq_pairs(r):
                rowb, qcol = (r // 4) * 6144, (r % 4) * 256
                out = []
                for j in range(4):
                    src = agout[rowb + 1536 * j:rowb + 1536 * j + 1024,
                                qcol:qcol + 256].rearrange("(t p) c -> p t c",
                                                           p=128)
                    out.append((qg.rearrange("p (t c) -> p t c", c=256)
                                [:, 8 * j:8 * j + 8, :], src))
                return out

            def _gk_pairs(r):
                rowb, kcol = (r // 4) * 6144, (r % 4) * 64
                return [(kg.rearrange("p (t v) -> p t v", v=64)
                         [:, 8 * j:8 * j + 8, :],
                         _kv_src(rowb + 1536 * j + 1024, kcol))
                        for j in range(4)]

            def _gv_pairs(r):
                rowb, kcol = (r // 4) * 6144, (r % 4) * 64
                return [(vgo.rearrange("p (t w) -> p t w", w=65)
                         [:, 8 * j:8 * j + 8, 0:64],
                         _kv_src(rowb + 1536 * j + 1280, kcol))
                        for j in range(4)]

            def delay_chain(eng, scratch, n, stream):
                tok = None
                for i in range(n):
                    src = agout[1536 * (i % 8) + 128:1536 * (i % 8) + 256, :]\
                        .rearrange("(a p) c -> p (a c)", p=128)
                    tok = sch.dma(eng, scratch, src,
                                  waits=[tok] if tok else [], stream=stream)
                return tok

            wb_arena = [apply_toks[7]] if mode == "full" else []
            dly = mode == "full"
            t_dk = delay_chain("Y", qT[:, 0:1024], 4, "gk") if dly else None
            t_dv = delay_chain("S", qT[:, 1024:2048], 8, "gv") if dly else None
            sch.dma_switch("G", _gq_pairs, 4, stream="gq")
            sch.dma_switch("Y", _gk_pairs, 4,
                           waits=wb_arena + ([t_dk] if t_dk else []),
                           stream="gk")
            sch.dma_switch("S", _gv_pairs, 4,
                           waits=wb_arena + ([t_dv] if t_dv else []),
                           stream="gv")

            for zz in range(4):
                sch.dma("G", xnT[:, 28704 + 1024 * zz:29728 + 1024 * zz],
                        cst["zeros_bf"][:, :], stream="gq")
            t_vones = sch.op("G", lambda e: e.tensor_copy(
                out=vgo.rearrange("p (t w) -> p t w", w=65)[:, :, 64:65],
                in_=ks["ones_bf"][:, 0:32].rearrange("p (c o) -> p c o", o=1)),
                waits=[t_constd], inc=True)
            t_gall = None
            for st_ in ("gq", "gk", "gv"):
                t_gall = ("d", st_, sch.d[st_])
                sch.wait("T", t_gall)
                sch.wait("V", t_gall)
            t_gall = ("d", "gq", sch.d["gq"])
            if dbg in (True, 1, 2, 6):
                for st_ in ("gq", "gk", "gv"):
                    sch.wait("G", ("d", st_, sch.d[st_]))
                sch.wait("G", ("m", "G", sch.m["G"]))
            if dbg in (True, 1, 6):
                dbq = nc.dram_tensor("dbg_qg", [128, 8192], BF16,
                                     kind="ExternalOutput").ap()
                dbk = nc.dram_tensor("dbg_kg", [128, 2048], BF16,
                                     kind="ExternalOutput").ap()
                dbv = nc.dram_tensor("dbg_vgo", [128, 2080], BF16,
                                     kind="ExternalOutput").ap()
                sch.dma("G", dbq, qg, stream="dbg")
                sch.dma("G", dbk, kg, stream="dbg")
                sch.dma("G", dbv, vgo, stream="dbg")
            if dbg == 6 and mode == "full":
                dag1 = nc.dram_tensor("dbg_ag1", [12288, CH], BF16,
                                      kind="ExternalOutput").ap()
                for r0 in range(0, 12288, 2048):
                    sch.dma("G", dag1[r0:r0 + 2048, :],
                            agout[r0:r0 + 2048, :], stream="dbg")

            t_ab = sch.op("S", lambda e: e.activation(
                out=stt[:, 40:41], in_=kts[:, 0:1], func=AF.Abs, scale=0.125),
                waits=[t_constd], inc=True)
            t_mneg = sch.op("V", lambda e: e.tensor_scalar(
                out=stt[:, 41:42], in0=stt[:, 40:41], scalar1=-1.0, scalar2=None,
                op0=ALU.mult), waits=[t_ab], inc=True)
            mneg = stt[:, 41:42]

            tp_ev = {}
            for grp in range(8):
                pb_ = grp % 2
                for dst_i, dstQ in ((0, QT01), (1, QT23)):
                    bw = sch.bank_waits([pb_ + 2 * dst_i])
                    for i in range(4):
                        tt = grp * 4 + i
                        sch.op("T", lambda e, tt=tt, i=i, pb_=pb_, dst_i=dst_i: e.matmul(
                            out=ps[:, 512 * (pb_ + 2 * dst_i) + 128 * i:
                                   512 * (pb_ + 2 * dst_i) + 128 * (i + 1)],
                            lhsT=qg[:, tt * 256 + 128 * dst_i:tt * 256 + 128 * (dst_i + 1)],
                            rhs=ks["ident"][:, :], start=True, stop=True),
                            waits=(bw + [t_gall]) if i == 0 else [],
                            inc=(i == 3))
                    t_t = ("m", "T", sch.m["T"])
                    ev = sch.op("V", lambda e, grp=grp, pb_=pb_, dst_i=dst_i,
                                dstQ=dstQ: e.tensor_copy(
                        out=dstQ[:, grp * 512:grp * 512 + 512],
                        in_=bank(pb_ + 2 * dst_i)[:, :]), waits=[t_t], inc=True)
                    sch.bank_read([pb_ + 2 * dst_i], ev)
                    tp_ev[(grp, dst_i)] = ev
            for grp in range(8):
                pb_ = 4 + grp % 2
                bw = sch.bank_waits([pb_])
                for i in range(4):
                    tt = grp * 4 + i
                    sch.op("T", lambda e, tt=tt, i=i, pb_=pb_: e.matmul(
                        out=ps[:, 512 * pb_ + 128 * i:512 * pb_ + 128 * (i + 1)][0:64, :],
                        lhsT=kg[:, tt * 64:(tt + 1) * 64],
                        rhs=ks["ident"][:, :], start=True, stop=True),
                        waits=(bw + [t_gall]) if i == 0 else [])
                    sch.op("T", lambda e, tt=tt, i=i, pb_=pb_: e.matmul(
                        out=ps[:, 512 * pb_ + 128 * i:512 * pb_ + 128 * (i + 1)][64:128, :],
                        lhsT=kg[:, tt * 64:(tt + 1) * 64],
                        rhs=ks["ident"][:, :], start=True, stop=True,
                        tile_position=(0, 64)), inc=(i == 3))
                t_t = ("m", "T", sch.m["T"])
                ev = sch.op("V", lambda e, grp=grp, pb_=pb_: e.tensor_copy(
                    out=KT2[:, grp * 512:(grp + 1) * 512],
                    in_=bank(pb_)[:, :]), waits=[t_t], inc=True)
                sch.bank_read([pb_], ev)
                tp_ev[("k", grp)] = ev
            t_tpall = ("m", "V", sch.m["V"])

            exp_tok = {}
            avdone = {}
            out_dma = {}
            rec_tok = None
            for qt in range(8):
                nkb = 4 * qt + 4
                for kb in range(nkb + 1):
                    if kb < nkb:
                        ql0 = max(0, kb * 128 - qt * 512)
                        diag = kb >= 4 * qt
                        for pair, (QTp, hh) in enumerate(((QT01, 0), (QT23, 2))):
                            w = sch.bank_waits([hh, hh + 1])
                            for h in (hh, hh + 1):
                                w += ([exp_tok[(qt, kb - 1, h)]] if (qt, kb - 1, h) in exp_tok else [])
                            if qt == 0 and kb == 0 and pair == 0:
                                w.append(t_tpall)
                            for i2, h in enumerate((hh, hh + 1)):
                                base_p = 64 * i2
                                sch.op("T", lambda e, kb=kb, qt=qt, ql0=ql0, h=h,
                                       base_p=base_p, QTp=QTp: e.matmul(
                                    out=bank(h)[:, ql0:512],
                                    lhsT=KT2[base_p:base_p + 64, kb * 128:(kb + 1) * 128],
                                    rhs=QTp[base_p:base_p + 64,
                                            qt * 512 + ql0:qt * 512 + 512],
                                    start=True, stop=True,
                                    tile_position=(base_p, 0)),
                                    waits=w if i2 == 0 else [], inc=(i2 == 1))
                            t_sc = ("m", "T", sch.m["T"])
                            for i2, h in enumerate((hh, hh + 1)):
                                slot = (h * 2 + (kb % 2)) * 512
                                tk = sch.op("S", lambda e, h=h, ql0=ql0, slot=slot: e.activation(
                                    out=att[:, slot + ql0:slot + 512],
                                    in_=bank(h)[:, ql0:512], func=AF.Exp,
                                    scale=0.125, bias=mneg),
                                    waits=[t_sc] + ([avdone[(qt, kb - 2)]]
                                                    if (qt, kb - 2) in avdone and h == hh else []),
                                    inc=True)
                                sch.bank_read([h], tk)
                                if diag:
                                    tk = sch.op("V", lambda e, h=h, ql0=ql0, slot=slot: e.tensor_tensor(
                                        out=att[:, slot + ql0:slot + ql0 + 128],
                                        in0=att[:, slot + ql0:slot + ql0 + 128],
                                        in1=ks["stair"][:, :], op=ALU.mult),
                                        waits=[tk], inc=True)
                                exp_tok[(qt, kb, h)] = tk
                    if kb >= 1:
                        kbp = kb - 1
                        ql0p = max(0, kbp * 128 - qt * 512)
                        for h in range(4):
                            slot = (h * 2 + (kbp % 2)) * 512
                            w = [exp_tok[(qt, kbp, h)]]
                            if kbp == 0:
                                w += sch.bank_waits([4 + h]) + [t_vones]
                            sch.op("T", lambda e, h=h, ql0p=ql0p, slot=slot,
                                   kbp=kbp, nkb=nkb: e.matmul(
                                out=bank(4 + h)[0:65, ql0p:512],
                                lhsT=vgo[:, kbp * 65:kbp * 65 + 65],
                                rhs=att[:, slot + ql0p:slot + 512],
                                start=(kbp == 0), stop=(kbp == nkb - 1)),
                                waits=w, inc=(h == 3))
                        avdone[(qt, kbp)] = ("m", "T", sch.m["T"])
                t_av = avdone[(qt, nkb - 1)]
                evtoks = []
                for h in range(4):
                    w = [t_av]
                    if rec_tok is not None and h == 0:
                        w.append(rec_tok)
                    evtoks.append(sch.op("V", lambda e, h=h, qt=qt: e.tensor_copy(
                        out=rec_f[64:65, 512 * h:512 * h + 512],
                        in_=bank(4 + h)[64:65, :]), waits=w, inc=True))
                t_rc = sch.op("V", lambda e: e.reciprocal(
                    out=rec_f[64:65, :], in_=rec_f[64:65, :]),
                    waits=[evtoks[-1]], inc=True)
                for h in range(4):
                    bw = sch.bank_waits([h])
                    t_bcm = sch.op("T", lambda e, h=h: e.matmul(
                        out=bank(h)[0:64, :], lhsT=ks["e64"][:, :],
                        rhs=rec_f[:, 512 * h:512 * h + 512].bitcast(BF16) if False
                        else rec_f[:, 512 * h:512 * h + 512],
                        start=True, stop=True), waits=bw + [t_rc], inc=True)
                    t_bce = sch.op("V", lambda e, h=h: e.tensor_copy(
                        out=bc_f[0:64, 512 * h:512 * h + 512],
                        in_=bank(h)[0:64, :]), waits=[t_bcm], inc=True)
                    w2 = [t_bce]
                    if (qt - 1, h) in out_dma:
                        w2.append(out_dma[(qt - 1, h)])
                    t_ob = sch.op("V", lambda e, h=h: e.tensor_tensor(
                        out=ob_out[0:64, 512 * h:512 * h + 512],
                        in0=bank(4 + h)[0:64, :],
                        in1=bc_f[0:64, 512 * h:512 * h + 512], op=ALU.mult),
                        waits=w2, inc=True)
                    sch.bank_read([h], t_ob)
                    sch.bank_read([4 + h], t_ob)
                    out_dma[(qt, h)] = sch.dma(
                        "G", attn_out[64 * h:64 * (h + 1), qt * 512:(qt + 1) * 512],
                        ob_out[0:64, 512 * h:512 * h + 512], waits=[t_ob],
                        stream=f"ao{h}")
                rec_tok = ("m", "T", sch.m["T"])
            if mode == "B":
                for st_ in ("ao0", "ao1", "ao2", "ao3"):
                    sch.wait("Y", ("d", st_, sch.d[st_]))

        if mode == "full":
            for st_ in ("ao0", "ao1", "ao2", "ao3"):
                sch.wait("G", ("d", st_, sch.d[st_]))

            sch.local_barrier()

            def _cc2(e):
                e.collective_compute(
                    "AllGather", ALU.bypass,
                    replica_groups=[[0, 1, 2, 3, 4, 5, 6, 7]],
                    ins=[attn_out], outs=[ag2],
                ).then_inc(cc2_sem)
            sch.raw("G", _cc2)

            def _fl2(e):
                e.collective_compute(
                    "AllGather", ALU.bypass,
                    replica_groups=[[0, 1, 2, 3, 4, 5, 6, 7]],
                    ins=[fl_in], outs=[fl_out2],
                ).then_inc(cc2_sem)
            sch.raw("G", _fl2)
            for eng in ENGS:
                sch.raw(eng, lambda e: e.wait_ge(cc2_sem, 2))

        if mode in ("C", "full"):
            dsl = bass.ds
            latT = xnT[:, 0:8192]
            ybuf = xnT[:, 8192:12288]
            ysq = xnT[:, 12288:13312]
            srow = xnT[0:1, 13312:21504].bitcast(F32)
            hnT = xnT[:, 0:32768]
            hnT3 = hnT.rearrange("p (kb c) -> p kb c", c=1024)
            fbuf = qT[:, 0:2048]
            fsq = qT[:, 2048:3072]
            frow = srow
            zsel = qT[:, 7168:8192]
            gza_bf = vT[:, 0:2048]
            ohT_sb = kT[0:4, 0:1024]
            oh_tok = kT[:, 1024:1056]
            eb_bf = kT[0:1, 1056:1120]
            xpc = x2t[:, 0:1024]
            x2pc = x2t[:, 1024:2048]
            outpc = x2t[:, 2048:3072]
            tpc = x2t[:, 3072:4096]
            bst_bf = qT[:, 3072:7168]
            gz_sb = qT[0:64, 0:2048].bitcast(F32)

            def _lat_pairs(r):
                row2, col2 = (r // 4) * 1024, (r % 4) * 1024
                src = ag2[row2:row2 + 1024, col2:col2 + 1024].rearrange(
                    "(kb p) c -> p kb c", p=128)
                return [(latT.rearrange("p (kb c) -> p kb c", c=1024), src)]
            t_dl = None
            if mode == "full":
                for i in range(8):
                    t_dl = sch.dma("G", qT[:, 2048:3072],
                                   ag2[256 * (i % 8):256 * (i % 8) + 128,
                                       0:1024],
                                   waits=[t_dl] if t_dl else [], stream="g2")
            t_lat = sch.dma_switch("G", _lat_pairs, 1,
                                   waits=[t_dl] if t_dl else [], stream="g2")
            if dbg in (True, 1, 3) and mode == "full":
                dbl = nc.dram_tensor("dbg_lat", [128, 8192], BF16,
                                     kind="ExternalOutput").ap()
                sch.dma("G", dbl, latT, waits=[t_lat], stream="dbg")
            if dbg in (5, 6) and mode == "full":
                din1 = nc.dram_tensor("dbg_ccin", [256, S], BF16,
                                      kind="ExternalOutput").ap()
                dbl = nc.dram_tensor("dbg_lat", [128, 8192], BF16,
                                     kind="ExternalOutput").ap()
                sch.dma("G", din1, attn_out, waits=[t_lat], stream="dbg")
                sch.dma("G", dbl, latT, stream="dbg")
            if dbg == 6 and mode == "full":
                dag2 = nc.dram_tensor("dbg_ag2", [2048, S], BF16,
                                      kind="ExternalOutput").ap()
                sch.dma("G", dag2, ag2, waits=[t_lat], stream="dbg")

            def wload(width, src_ap, cast_dst, rows=None, dst3=None):
                i = wseq[0]
                wseq[0] += 1
                sl = (i % 2) * D
                dst = wf[:, sl:sl + width] if rows is None else wf[rows[0]:rows[1], sl:sl + width]
                ddst = dst if dst3 is None else dst.rearrange(
                    "p (a c) -> p a c", a=dst3)
                tok_d = sch.dma("Y", ddst, src_ap,
                                waits=([cast_toks[i - 2]] if i - 2 in cast_toks
                                       else [t_c1w]),
                                stream=f"w{i % 2}")
                cast_toks[i] = sch.op("G", lambda e: e.tensor_copy(
                    out=cast_dst,
                    in_=dst), waits=[tok_d, ("m", "T", sch.m["T"])], inc=True)
                return cast_toks[i]

            t_gzac = wload(2048, gza_d.rearrange("(kb p) c -> p kb c", p=128),
                           gza_bf[:, :], dst3=32)
            t_ebc = wload(64, eb_d[:, :], eb_bf[:, :], rows=(0, 1))
            t_bstc = wload(4096, bst_d[:, :], bst_bf[32:64, :], rows=(32, 64))

            yev = {}
            for ob in range(32):
                i = wseq[0]
                t_cast = wchunk(i, 1024,
                                wo_d[ob].rearrange("(kb p) c -> p kb c", p=128),
                                ("m", "T", sch.m["T"]), dst3=(8, 128))
                wseq[0] += 1
                pl = (ob % 2) * 2
                bw = sch.bank_waits([pl, pl + 1])
                for kb in range(8):
                    for piece in range(2):
                        sch.op("T", lambda e, i=i, kb=kb, piece=piece, pl=pl: e.matmul(
                            out=bank(pl + piece)[:, :],
                            lhsT=wb[:, (i % 2) * D + 128 * kb:(i % 2) * D + 128 * (kb + 1)],
                            rhs=latT[:, 1024 * kb + 512 * piece:1024 * kb + 512 * (piece + 1)],
                            start=(kb == 0), stop=(kb == 7)),
                            waits=(bw + [t_cast, t_lat]) if kb == 0 and piece == 0 else [],
                            inc=(kb == 7 and piece == 1))
                t_mm = ("m", "T", sch.m["T"])
                ysl = (ob % 2) * 1024
                t_ev = sch.op("V", lambda e, pl=pl, ysl=ysl: e.tensor_copy(
                    out=ybuf[:, ysl:ysl + 1024],
                    in_=ps[:, 512 * pl:512 * pl + 1024]),
                    waits=[t_mm] + ([yev[ob - 2]] if ob - 2 in yev else []), inc=True)
                sch.bank_read([pl, pl + 1], t_ev)
                t_sq = sch.op("S", lambda e, ysl=ysl: e.activation(
                    out=ysq[:, :], in_=ybuf[:, ysl:ysl + 1024], func=AF.Square),
                    waits=[t_ev] + ([stat_tok[ob - 1]] if ob - 1 in stat_tok else []),
                    inc=True)
                bw2 = sch.bank_waits([4, 5, 6, 7]) if ob == 0 else []
                for piece in range(2):
                    sch.op("T", lambda e, piece=piece, ysl=ysl, ob=ob: e.matmul(
                        out=bank(4 + piece)[0:1, :], lhsT=ks["ones_bf"][:, 0:1],
                        rhs=ybuf[:, ysl + 512 * piece:ysl + 512 * (piece + 1)],
                        start=(ob == 0), stop=(ob == 31)),
                        waits=(bw2 + [t_ev]) if piece == 0 else [])
                for piece in range(2):
                    sch.op("T", lambda e, piece=piece, ob=ob: e.matmul(
                        out=bank(6 + piece)[0:1, :], lhsT=ks["ones_bf"][:, 0:1],
                        rhs=ysq[:, 512 * piece:512 * (piece + 1)],
                        start=(ob == 0), stop=(ob == 31)),
                        waits=[t_sq], inc=(piece == 1))
                stat_tok[ob] = ("m", "T", sch.m["T"])
                yev[ob] = sch.dma("G", wyd[128 * ob:128 * (ob + 1), :],
                                  ybuf[:, ysl:ysl + 1024], waits=[t_ev],
                                  stream=f"sy{ob % 2}")

            def row_stats(sum_banks, sq_banks, row, lanes, wtok):
                evs = []
                for piece in range(2):
                    evs.append(sch.op("V", lambda e, piece=piece: e.tensor_copy(
                        out=row[:, 512 * piece:512 * (piece + 1)],
                        in_=bank(sum_banks[piece])[0:1, :]), waits=[wtok],
                        inc=True))
                    evs.append(sch.op("V", lambda e, piece=piece: e.tensor_copy(
                        out=row[:, 1024 + 512 * piece:1024 + 512 * (piece + 1)],
                        in_=bank(sq_banks[piece])[0:1, :]), inc=True))
                for b_ in sum_banks + sq_banks:
                    sch.bank_read([b_], evs[-1])
                t1 = sch.op("V", lambda e: e.tensor_scalar(
                    out=row[:, 2048:3072], in0=row[:, 0:1024], scalar1=1.0 / D,
                    scalar2=None, op0=ALU.mult), waits=[evs[-1]], inc=True)
                t2 = sch.op("V", lambda e: e.tensor_tensor(
                    out=row[:, 3072:4096], in0=row[:, 2048:3072],
                    in1=row[:, 2048:3072], op=ALU.mult), waits=[t1], inc=True)
                t3 = sch.op("V", lambda e: e.tensor_scalar(
                    out=row[:, 0:1024], in0=row[:, 1024:2048], scalar1=1.0 / D,
                    scalar2=None, op0=ALU.mult), waits=[t2], inc=True)
                t4 = sch.op("V", lambda e: e.tensor_tensor(
                    out=row[:, 0:1024], in0=row[:, 0:1024],
                    in1=row[:, 3072:4096], op=ALU.subtract), waits=[t3], inc=True)
                t5 = sch.op("S", lambda e: e.activation(
                    out=row[:, 0:1024], in_=row[:, 0:1024], func=AF.Ln,
                    bias=ks["fconst"][0:1, 0:1]), waits=[t4], inc=True)
                t6 = sch.op("S", lambda e: e.activation(
                    out=row[:, 0:1024], in_=row[:, 0:1024], func=AF.Exp,
                    scale=-0.5), waits=[t5], inc=True)
                t7 = sch.op("V", lambda e: e.tensor_scalar(
                    out=row[:, 2048:3072], in0=row[:, 2048:3072], scalar1=-1.0,
                    scalar2=None, op0=ALU.mult), waits=[t6], inc=True)
                bw = sch.bank_waits([0])
                for sub in range(8):
                    sch.op("T", lambda e, sub=sub: e.matmul(
                        out=bank(0)[:, 2 * sub:2 * sub + 1],
                        lhsT=row[:, 2048 + 128 * sub:2048 + 128 * (sub + 1)],
                        rhs=ks["fconst"][0:1, 1:2], start=True, stop=True),
                        waits=(bw + [t7]) if sub == 0 else [])
                    sch.op("T", lambda e, sub=sub: e.matmul(
                        out=bank(0)[:, 2 * sub + 1:2 * sub + 2],
                        lhsT=row[:, 128 * sub:128 * (sub + 1)],
                        rhs=ks["fconst"][0:1, 1:2], start=True, stop=True),
                        inc=(sub == 7))
                t8 = ("m", "T", sch.m["T"])
                t9 = sch.op("V", lambda e: e.tensor_copy(
                    out=stt[:, lanes:lanes + 16].rearrange(
                        "p (s two) -> p s two", two=2) if False else stt[:, lanes:lanes + 16],
                    in_=bank(0)[:, 0:16]), waits=[t8], inc=True)
                sch.bank_read([0], t9)
                return t9

            stat_tok = stat_tok
            t_ln2 = row_stats([4, 5], [6, 7], srow, 0, stat_tok[31])

            def combine_pass(ydram, row_lanes, res_src, res_rows_off, dst_dram,
                             stats_tok, rd_stream, xs_stream, out_stream,
                             slot_free0):
                rdev = {}
                cmb = {}
                xslot_tok = {}
                for obg in range(8):
                    for obi in range(4):
                        ob = obg * 4 + obi
                        ysl = (ob % 2) * 1024
                        w = [stats_tok]
                        if ob - 2 in rdev:
                            w.append(rdev[ob - 2])
                        elif ob < 2:
                            w.append(slot_free0[ob % 2])
                        t_rd = sch.dma("Y", ybuf[:, ysl:ysl + 1024],
                                       ydram[128 * ob:128 * (ob + 1), :],
                                       waits=w, stream=f"{rd_stream}{ob % 2}")
                        for sub in range(8):
                            bw = sch.bank_waits([sub]) if obi == 0 else []
                            rdev[ob] = sch.op("T", lambda e, ob=ob, obi=obi,
                                              sub=sub, ysl=ysl: e.matmul(
                                out=bank(sub)[:, 128 * obi:128 * (obi + 1)],
                                lhsT=ybuf[:, ysl + 128 * sub:ysl + 128 * (sub + 1)],
                                rhs=ks["ident"][:, :], start=True, stop=True),
                                waits=(bw + [t_rd]) if True else [],
                                inc=(sub == 7))
                    t_tp = ("m", "T", sch.m["T"])
                    for sub in range(8):
                        xsl = (sub % 2) * 512
                        w = []
                        if xslot_tok.get(sub % 2) is not None:
                            w.append(xslot_tok[sub % 2])
                        t_x = sch.dma("Y", xpc[:, xsl:xsl + 512],
                                      res_src[res_rows_off + 128 * sub:
                                              res_rows_off + 128 * (sub + 1),
                                              512 * obg:512 * (obg + 1)],
                                      waits=w, stream=f"{xs_stream}{sub % 2}")
                        t_af = sch.op("V", lambda e, sub=sub, xsl=xsl: e.tensor_scalar(
                            out=tpc[:, xsl:xsl + 512], in0=bank(sub)[:, :],
                            scalar1=stt[:, row_lanes + 2 * sub:row_lanes + 2 * sub + 1],
                            scalar2=stt[:, row_lanes + 2 * sub + 1:row_lanes + 2 * sub + 2],
                            op0=ALU.add, op1=ALU.mult),
                            waits=[t_tp] + w, inc=True)
                        sch.bank_read([sub], t_af)
                        t_cm = sch.op("V", lambda e, sub=sub, xsl=xsl: e.tensor_tensor(
                            out=outpc[:, xsl:xsl + 512], in0=tpc[:, xsl:xsl + 512],
                            in1=xpc[:, xsl:xsl + 512], op=ALU.add),
                            waits=[t_af, t_x], inc=True)
                        cmb[(obg, sub)] = sch.dma(
                            "G", dst_dram[128 * sub:128 * (sub + 1),
                                          512 * obg:512 * (obg + 1)],
                            outpc[:, xsl:xsl + 512], waits=[t_cm],
                            stream=f"{out_stream}{sub % 2}")
                        xslot_tok[sub % 2] = cmb[(obg, sub)]
                return cmb

            cmb2 = combine_pass(wyd, 0, xs, 4, x2d, t_ln2, "ry", "rx", "sx",
                                {0: yev[30], 1: yev[31]})
            t_x2_all = [cmb2[(7, s)] for s in range(8)]

            ap3_toks = {}
            h_slot_tok = {}
            for t in range(8):
                L = 24 + 8 * (t % 2)
                sl0 = (t % 2) * D
                xt = x2t[0:128, sl0:sl0 + D]
                xo = xn_bf[0:128, sl0:sl0 + D]
                w = [cmb2[(g, t)] for g in range(8)]
                if t < 2:
                    w += [cmb2[(7, s)] for s in range(8)]
                if t - 2 in ap3_toks:
                    w.append(ap3_toks[t - 2])
                t_xr = sch.dma("Y", xt, x2d[128 * t:128 * (t + 1), :], waits=w,
                               stream=f"x2r{t % 2}")
                slot_free = h_slot_tok.get(t % 2)
                t_sq = sch.op("S", lambda e, xo=xo, xt=xt, L=L: e.activation(
                    out=xo, in_=xt, func=AF.Square, accum_out=stt[:, L + 1:L + 2]),
                    waits=[t_xr] + ([slot_free] if slot_free else []), inc=True)
                t_red = sch.op("V", lambda e, xt=xt, L=L: e.tensor_reduce(
                    out=stt[:, L:L + 1], in_=xt, axis=AX.X, op=ALU.add),
                    waits=[t_xr], inc=True)
                t_m = sch.op("V", lambda e, L=L: e.tensor_scalar(
                    out=stt[:, L + 2:L + 3], in0=stt[:, L:L + 1],
                    scalar1=1.0 / D, scalar2=None, op0=ALU.mult),
                    waits=[t_red], inc=True)
                t_msq = sch.op("V", lambda e, L=L: e.tensor_tensor(
                    out=stt[:, L + 3:L + 4], in0=stt[:, L + 2:L + 3],
                    in1=stt[:, L + 2:L + 3], op=ALU.mult), waits=[t_m], inc=True)
                t_var = sch.op("V", lambda e, L=L: e.tensor_scalar(
                    out=stt[:, L + 4:L + 5], in0=stt[:, L + 1:L + 2],
                    scalar1=1.0 / D, scalar2=stt[:, L + 3:L + 4],
                    op0=ALU.mult, op1=ALU.subtract), waits=[t_sq, t_msq], inc=True)
                t_ln = sch.op("S", lambda e, L=L: e.activation(
                    out=stt[:, L + 5:L + 6], in_=stt[:, L + 4:L + 5],
                    func=AF.Ln, bias=ks["fconst"][:, 0:1]), waits=[t_var], inc=True)
                t_rs = sch.op("S", lambda e, L=L: e.activation(
                    out=stt[:, L + 6:L + 7], in_=stt[:, L + 5:L + 6],
                    func=AF.Exp, scale=-0.5), waits=[t_ln], inc=True)
                t_mn = sch.op("V", lambda e, L=L: e.tensor_scalar(
                    out=stt[:, L + 7:L + 8], in0=stt[:, L + 2:L + 3],
                    scalar1=-1.0, scalar2=None, op0=ALU.mult), waits=[t_m], inc=True)
                t_ap = sch.op("V", lambda e, xo=xo, xt=xt, L=L: e.tensor_scalar(
                    out=xo, in0=xt, scalar1=stt[:, L + 7:L + 8],
                    scalar2=stt[:, L + 6:L + 7], op0=ALU.add, op1=ALU.mult),
                    waits=[t_rs, t_mn] + ([slot_free] if slot_free else []),
                    inc=True)
                ap3_toks[t] = t_ap
                for g in range(8):
                    pbank = 6 + (g % 2)
                    bw = sch.bank_waits([pbank])
                    for i in range(4):
                        kb = 4 * g + i
                        t_tp = sch.op("T", lambda e, kb=kb, i=i, pbank=pbank,
                                      sl0=sl0, xo=xo: e.matmul(
                            out=ps[:, 512 * pbank + 128 * i:512 * pbank + 128 * (i + 1)],
                            lhsT=xn_bf[0:128, sl0 + 128 * kb:sl0 + 128 * (kb + 1)],
                            rhs=ks["ident"][:, :], start=True, stop=True),
                            waits=(bw + [t_ap, t_ln2]) if i == 0 else [],
                            inc=(i == 3))
                    t_tpm = ("m", "T", sch.m["T"])
                    ev = sch.op("V", lambda e, g=g, t=t, pbank=pbank: e.tensor_copy(
                        out=hnT3[:, 4 * g:4 * g + 4, 128 * t:128 * (t + 1)],
                        in_=bank(pbank)[:, 0:512]), waits=[t_tpm], inc=True)
                    sch.bank_read([pbank], ev)
                h_slot_tok[t % 2] = ("m", "T", sch.m["T"])
            t_hnT = ("m", "V", sch.m["V"])

            bwg = sch.bank_waits([6, 7])
            for kb in range(32):
                for piece in range(2):
                    sch.op("T", lambda e, kb=kb, piece=piece: e.matmul(
                        out=bank(6 + piece)[0:64, :],
                        lhsT=gza_bf[:, 64 * kb:64 * (kb + 1)],
                        rhs=hnT3[:, kb, 512 * piece:512 * (piece + 1)],
                        start=(kb == 0), stop=False),
                        waits=(bwg + [t_gzac, t_hnT]) if kb == 0 and piece == 0 else [])
            for piece in range(2):
                sch.op("T", lambda e, piece=piece: e.matmul(
                    out=bank(6 + piece)[0:64, :], lhsT=eb_bf[:, :],
                    rhs=ks["ones_bf"][0:1, 512 * piece:512 * (piece + 1)],
                    start=False, stop=True), waits=[t_ebc], inc=(piece == 1))
            t_gmm = ("m", "T", sch.m["T"])
            tz1 = sch.op("V", lambda e: e.tensor_copy(
                out=gz_sb[:, 0:512], in_=bank(6)[0:64, :]), waits=[t_gmm])
            tz2 = sch.op("V", lambda e: e.tensor_copy(
                out=gz_sb[:, 512:1024], in_=bank(7)[0:64, :]), waits=[t_gmm],
                inc=True)
            sch.bank_read([6, 7], tz2)
            bw6 = sch.bank_waits([6])
            for sub in range(8):
                sch.op("T", lambda e, sub=sub: e.matmul(
                    out=bank(6)[:, 4 * sub:4 * (sub + 1)],
                    lhsT=gz_sb[0:4, 128 * sub:128 * (sub + 1)],
                    rhs=ks["ident4"][:, :], start=True, stop=True),
                    waits=(bw6 + [tz2]) if sub == 0 else [], inc=(sub == 7))
            t_lt = ("m", "T", sch.m["T"])
            mx_toks = []
            for sub in range(8):
                tk = sch.op("V", lambda e, sub=sub: e.tensor_reduce(
                    out=stt[:, 42:43],
                    in_=bank(6)[:, 4 * sub:4 * (sub + 1)], axis=AX.X, op=ALU.max),
                    waits=[t_lt] + (mx_toks[-1:]), inc=True)
                tk = sch.op("V", lambda e, sub=sub: e.tensor_scalar(
                    out=oh_tok[:, 4 * sub:4 * (sub + 1)],
                    in0=bank(6)[:, 4 * sub:4 * (sub + 1)],
                    scalar1=stt[:, 42:43], scalar2=None, op0=ALU.is_ge),
                    waits=[tk], inc=True)
                mx_toks.append(tk)
            t_oh = mx_toks[-1]
            sch.bank_read([6], t_oh)
            bw67 = sch.bank_waits([6, 7])
            for sub in range(8):
                sch.op("T", lambda e, sub=sub: e.matmul(
                    out=bank(6 + sub // 4)[0:4, 128 * (sub % 4):128 * (sub % 4 + 1)],
                    lhsT=oh_tok[:, 4 * sub:4 * (sub + 1)],
                    rhs=ks["ident"][:, :], start=True, stop=True),
                    waits=(bw67 + [t_oh]) if sub == 0 else [], inc=(sub == 7))
            t_oht = ("m", "T", sch.m["T"])
            toe = sch.op("V", lambda e: e.tensor_copy(
                out=ohT_sb[:, 0:512], in_=bank(6)[0:4, :]), waits=[t_oht])
            toe = sch.op("V", lambda e: e.tensor_copy(
                out=ohT_sb[:, 512:1024], in_=bank(7)[0:4, :]), inc=True)
            sch.bank_read([6, 7], toe)
            bw6b = sch.bank_waits([6, 7])
            for piece in range(2):
                sch.op("T", lambda e, piece=piece: e.matmul(
                    out=bank(6 + piece)[32:64, :], lhsT=ks["repmat"][:, :],
                    rhs=ohT_sb[:, 512 * piece:512 * (piece + 1)],
                    start=True, stop=True, tile_position=(0, 32)),
                    waits=(bw6b + [toe]) if piece == 0 else [], inc=(piece == 1))
            t_rep = ("m", "T", sch.m["T"])
            tzs = None
            for piece in range(2):
                tzs = sch.op("V", lambda e, piece=piece: e.tensor_tensor(
                    out=zsel[32:64, 512 * piece:512 * (piece + 1)],
                    in0=gz_sb[32:64, 512 * piece:512 * (piece + 1)],
                    in1=bank(6 + piece)[32:64, :], op=ALU.mult),
                    waits=[t_rep], inc=True)
            t_zsel = tzs
            sch.bank_read([6, 7], t_zsel)

            fev = {}
            fstat = {}
            for ob in range(32):
                i = wseq[0]
                t_cast = wchunk(i, D,
                                wup_d[ob].rearrange("(kb p) c -> p kb c", p=128),
                                ("m", "T", sch.m["T"]), dst3=(32, 128))
                wseq[0] += 1
                pl = (ob % 2) * 2
                bw = sch.bank_waits([pl, pl + 1])
                for kb in range(32):
                    for piece in range(2):
                        sch.op("T", lambda e, i=i, kb=kb, piece=piece, pl=pl: e.matmul(
                            out=bank(pl + piece)[:, :],
                            lhsT=wb[:, (i % 2) * D + 128 * kb:(i % 2) * D + 128 * (kb + 1)],
                            rhs=hnT3[:, kb, 512 * piece:512 * (piece + 1)],
                            start=(kb == 0), stop=False),
                            waits=(bw + [t_cast, t_hnT]) if kb == 0 and piece == 0 else [])
                for piece in range(2):
                    sch.op("T", lambda e, ob=ob, piece=piece, pl=pl: e.matmul(
                        out=bank(pl + piece)[:, :],
                        lhsT=bst_bf[32:64, 128 * ob:128 * (ob + 1)],
                        rhs=zsel[32:64, 512 * piece:512 * (piece + 1)],
                        start=False, stop=True),
                        waits=[t_zsel, t_bstc] if piece == 0 else [],
                        inc=(piece == 1))
                t_mm = ("m", "T", sch.m["T"])
                fsl = (ob % 2) * 1024
                t_ev = sch.op("V", lambda e, pl=pl, fsl=fsl: e.tensor_copy(
                    out=fbuf[:, fsl:fsl + 1024],
                    in_=ps[:, 512 * pl:512 * pl + 1024]),
                    waits=[t_mm, t_zsel] + ([fev[ob - 2]] if ob - 2 in fev else []),
                    inc=True)
                sch.bank_read([pl, pl + 1], t_ev)
                t_sq = sch.op("S", lambda e, fsl=fsl: e.activation(
                    out=fsq[:, :], in_=fbuf[:, fsl:fsl + 1024], func=AF.Square),
                    waits=[t_ev] + ([fstat[ob - 1]] if ob - 1 in fstat else []),
                    inc=True)
                bw2 = sch.bank_waits([4, 5, 6, 7]) if ob == 0 else []
                for piece in range(2):
                    sch.op("T", lambda e, piece=piece, fsl=fsl, ob=ob: e.matmul(
                        out=bank(4 + piece)[0:1, :], lhsT=ks["ones_bf"][:, 0:1],
                        rhs=fbuf[:, fsl + 512 * piece:fsl + 512 * (piece + 1)],
                        start=(ob == 0), stop=(ob == 31)),
                        waits=(bw2 + [t_ev]) if piece == 0 else [])
                for piece in range(2):
                    sch.op("T", lambda e, piece=piece, ob=ob: e.matmul(
                        out=bank(6 + piece)[0:1, :], lhsT=ks["ones_bf"][:, 0:1],
                        rhs=fsq[:, 512 * piece:512 * (piece + 1)],
                        start=(ob == 0), stop=(ob == 31)),
                        waits=[t_sq], inc=(piece == 1))
                fstat[ob] = ("m", "T", sch.m["T"])
                fev[ob] = sch.dma("G", fyd[128 * ob:128 * (ob + 1), :],
                                  fbuf[:, fsl:fsl + 1024], waits=[t_ev],
                                  stream=f"sf{ob % 2}")

            t_ln4 = row_stats([4, 5], [6, 7], frow, 0, fstat[31])
            cmb4 = combine_pass(fyd, 0, x2d, 0, xout, t_ln4, "rf", "r2", "xo",
                                {0: fev[30], 1: fev[31]})
            if dbg == 4:
                dwy = nc.dram_tensor("dbg_wy", [D, CH], BF16, kind="ExternalOutput").ap()
                dx2 = nc.dram_tensor("dbg_x2", [CH, D], F32, kind="ExternalOutput").ap()
                dfy = nc.dram_tensor("dbg_fy", [D, CH], BF16, kind="ExternalOutput").ap()
                w = [cmb4[(7, s_)] for s_ in range(8)]
                sch.dma("G", dwy, wyd[:, :], waits=w, stream="dbg")
                sch.dma("G", dx2, x2d[:, :], stream="dbg")
                sch.dma("G", dfy, fyd[:, :], stream="dbg")
                sch.wait("Y", ("d", "dbg", sch.d["dbg"]))
            if mode in ("C", "full"):
                for s_ in range(2):
                    sch.wait("Y", ("d", f"xo{s_}", sch.d[f"xo{s_}"]))

        blk = ctx.enter_context(nc.Block())

        @blk.tensor
        def _(e):
            for fn in sch.q["T"]:
                fn(e)

        @blk.vector
        def _(e):
            for fn in sch.q["V"]:
                fn(e)

        @blk.scalar
        def _(e):
            for fn in sch.q["S"]:
                fn(e)

        @blk.gpsimd
        def _(e):
            for fn in sch.q["G"]:
                fn(e)

        @blk.sync
        def _(e):
            for fn in sch.q["Y"]:
                fn(e)

    return nc


_BUILD_CACHE = {}


def _get_build(mode):
    if mode not in _BUILD_CACHE:
        _BUILD_CACHE[mode] = build(mode)
    return _BUILD_CACHE[mode]


def kernel(**inputs):
    cores = list(range(8))
    sh = prep_shared(inputs)
    nc = _get_build("full")
    in_maps = []
    for c in cores:
        m = dict(sh)
        m.update(prep_core(inputs, c))
        in_maps.append(m)
    res = run_bass_kernel_spmd(nc, in_maps, core_ids=cores)
    out = np.empty((NB, S, D), np.float32)
    for c in cores:
        b, j = c // 4, c % 4
        out[b, j * CH:(j + 1) * CH] = np.asarray(res.results[c]["xout"])
    return out
